# version 02
# baseline (speedup 1.0000x reference)
import numpy as np
from contextlib import ExitStack

import concourse.bass as bass
import concourse.mybir as mybir
from concourse.bass_utils import run_bass_kernel_spmd

F32 = mybir.dt.float32
AXX = mybir.AxisListType.X
ALU = mybir.AluOpType
ACTF = mybir.ActivationFunctionType

B, S, D = 4, 2048, 4096
HALF = 1024
NKV = 8
HD = 128
CKV = 512
RD = 512
INV_SQRT_HD = 0.08838834764831845
NEG1E9 = -1.0e9
EPS = 1.0e-5

ENGINES = ("sync", "gpsimd", "tensor", "vector", "scalar")


class Sched:

    def __init__(self):
        self.cnt = {"W": 0, "G": 0, "T": 0, "V": 0, "A": 0}
        self.stream = {e: [] for e in ENGINES}
        self.lastw = {e: {} for e in ENGINES}
        self.own = {"sync": "W", "gpsimd": "G", "tensor": "T",
                    "vector": "V", "scalar": "A"}

    def op(self, eng, emit, deps=(), sig=False):
        waits = {}
        for d in deps:
            if d is None:
                continue
            sm, v = d
            if v <= 0:
                continue
            if sm == self.own[eng] and sm == "T":
                continue
            if v > waits.get(sm, 0):
                waits[sm] = v
        wl = []
        for sm, v in sorted(waits.items()):
            if self.lastw[eng].get(sm, 0) >= v:
                continue
            self.lastw[eng][sm] = v
            wl.append((sm, v))
        tok = None
        if sig:
            sm = self.own[eng]
            amt = 16 if sm in ("W", "G") else 1
            self.cnt[sm] += amt
            tok = (sm, self.cnt[sm])
        self.stream[eng].append([wl, emit, tok])
        return tok

    def retro_sig(self, eng):
        st = self.stream[eng]
        if not st:
            return None
        if st[-1][2] is None:
            sm = self.own[eng]
            amt = 16 if sm in ("W", "G") else 1
            self.cnt[sm] += amt
            st[-1][2] = (sm, self.cnt[sm])
        return st[-1][2]

    def join(self):
        return [t for t in (self.retro_sig(e) for e in ENGINES) if t]


def build_program():
    nc = bass.Bass("TRN2", target_bir_lowering=False)

    def din(name, shape):
        return nc.dram_tensor(name, list(shape), F32, kind="ExternalInput").ap()

    xh = [din("x_h0", (HALF, D)), din("x_h1", (HALF, D))]
    x_own = din("x_own", (HALF, D))
    wqt = din("wqt", (D, D))
    wkrt = din("wkrt", (D, RD))
    wdkt = din("wdkt", (D, CKV))
    wupkt = din("wupkt", (CKV, CKV))
    wupvt = din("wupvt", (CKV, NKV * HD))
    wot = din("wot", (D, D))
    wft = din("wft", (D, D))
    bffn = din("bffn", (128, 32))
    identc = din("identc", (128, 128))
    diagm = din("diagm", (128, 128))
    pqc = din("pq", (128, 128))
    pkc = din("pk", (128, 128))
    onesc = din("onesc", (128, 128))
    tailc = din("tailmask", (128, 2))
    cosq_in = [din("cosq0", (128, 512)), din("cosq1", (128, 512))]
    sinq_in = [din("sinq0", (128, 512)), din("sinq1", (128, 512))]
    cosk_in = [din("cosk0", (128, HALF)), din("cosk1", (128, HALF))]
    sink_in = [din("sink0", (128, HALF)), din("sink1", (128, HALF))]

    out = nc.dram_tensor("out", [HALF, D], F32, kind="ExternalOutput").ap()

    q_sc = nc.dram_tensor("q_sc", [D, HALF], F32, kind="Internal").ap()
    k_sc = nc.dram_tensor("k_sc", [NKV, 128, S], F32, kind="Internal").ap()
    v_sc = nc.dram_tensor("v_sc", [NKV, S, HD], F32, kind="Internal").ap()
    ao_sc = nc.dram_tensor("ao_sc", [D, HALF], F32, kind="Internal").ap()
    x1_sc = nc.dram_tensor("x1_sc", [HALF, D], F32, kind="Internal").ap()

    with ExitStack() as ctx:
        def sbuf(name, shape, dtype=F32):
            return ctx.enter_context(nc.sbuf_tensor(name, shape, dtype))

        PH = sbuf("PH", [128, 32768])
        PW = sbuf("PW", [128, 7168])
        PX = sbuf("PX", [128, 8192])
        PC = sbuf("PC", [128, 3712])
        PV = sbuf("PV", [128, 512])
        sem = {x: ctx.enter_context(nc.semaphore("sem_" + x))
               for x in ("W", "G", "T", "V", "A")}
        PS1 = ctx.enter_context(nc.psum_tensor("PS1", [128, 2048], F32))
        PS2 = ctx.enter_context(nc.psum_tensor("PS2", [128, 2048], F32))

        IDENT = PC.ap()[:, 0:128]
        DIAG = PC.ap()[:, 128:256]
        PQM = PC.ap()[:, 256:384]
        PKM = PC.ap()[:, 384:512]
        ONES = PC.ap()[:, 512:640]
        COSQ = PC.ap()[:, 640:1152]
        SINQ = PC.ap()[:, 1152:1664]
        COSK = PC.ap()[:, 1664:2688]
        SINKD = PC.ap()[:, 2688:3712]

        s = Sched()
        V1 = PV.ap()

        class Vec:
            def __init__(self):
                self.n = 0

            def slot(self, w=1):
                a = self.n
                self.n += w
                assert self.n <= 494, "vec pool overflow"
                return V1[:, a:a + w]

        vec = Vec()
        v_zero = vec.slot()
        v_tails = vec.slot(2)
        v_eps = vec.slot()
        v_1285 = vec.slot()
        b_sb = vec.slot(32)
        v_acc = vec.slot()
        v_gmc = vec.slot()
        v_th = vec.slot()
        v_thn = vec.slot()
        rstd3 = [vec.slot() for _ in range(8)]
        bias3 = [vec.slot() for _ in range(8)]
        sinv3 = [vec.slot() for _ in range(8)]
        svec3 = [vec.slot() for _ in range(8)]
        v_m = V1[:, 494:495]
        v_sm = V1[:, 495:496]
        v_bi = V1[:, 496:497]
        v_rc = V1[:, 497:498]
        v_r3 = V1[:, 498:499]
        v_am = V1[:, 499:500]
        v_rcd = V1[:, 500:501]
        v_ssa = [V1[:, 501 + i:502 + i] for i in range(4)]

        def dma(queue, dst, src, deps=()):
            return s.op(queue,
                        (lambda e, d=dst, so=src: e.dma_start(out=d, in_=so)),
                        deps, sig=True)

        def mm(outap, lhsT, rhs, start, stop, deps=(), sig=False):
            return s.op("tensor",
                        (lambda e, o=outap, l=lhsT, r=rhs, st=start, sp=stop:
                         e.matmul(o, l, r, start=st, stop=sp)), deps, sig=sig)

        def tp(outap, inap, deps=(), sig=False):
            return s.op("tensor",
                        (lambda e, o=outap, i=inap: e.transpose(o, i, IDENT)),
                        deps, sig=sig)

        def vcopy(dst, src, deps=(), sig=False):
            return s.op("vector",
                        (lambda e, d=dst, so=src: e.tensor_copy(d, so)),
                        deps, sig=sig)

        def vtt(dst, a, bb, op, deps=(), sig=False):
            return s.op("vector",
                        (lambda e, d=dst, x=a, y=bb, o=op:
                         e.tensor_tensor(d, x, y, op=o)), deps, sig=sig)

        def vts(dst, a, s1, op, deps=(), sig=False):
            return s.op("vector",
                        (lambda e, d=dst, x=a, c=s1, o=op:
                         e.tensor_single_scalar(d, x, c, op=o)), deps, sig=sig)

        def vstt(dst, in0, sc, in1, op0, op1, deps=(), sig=False):
            return s.op("vector",
                        (lambda e, d=dst, a=in0, c=sc, bb=in1, o0=op0, o1=op1:
                         e.scalar_tensor_tensor(d, a, c, bb, op0=o0, op1=o1)),
                        deps, sig=sig)

        def vred(dst, src, op, deps=(), sig=False, absv=False, neg=False):
            return s.op("vector",
                        (lambda e, d=dst, so=src, o=op, ab=absv, ng=neg:
                         e.tensor_reduce(d, so, axis=AXX, op=o,
                                         apply_absolute_value=ab, negate=ng)),
                        deps, sig=sig)

        def vmemset(dst, val, deps=(), sig=False):
            return s.op("vector", (lambda e, d=dst, v=val: e.memset(d, v)),
                        deps, sig=sig)

        def vrecip(dst, src, deps=(), sig=False):
            return s.op("vector",
                        (lambda e, d=dst, so=src: e.reciprocal(d, so)),
                        deps, sig=sig)

        def act(dst, src, func, bias=None, scale=1.0, accum=None, deps=(),
                sig=False):
            bias_ap = v_zero if bias is None else bias

            def emitf(e, d=dst, so=src, f=func, bb=bias_ap, sc=scale, au=accum):
                if au is None:
                    return e.activation(d, so, f, bias=bb, scale=sc)
                return e.activation(d, so, f, bias=bb, scale=sc, accum_out=au)
            return s.op("scalar", emitf, deps, sig=sig)

        dma("sync", IDENT, identc)
        dma("sync", DIAG, diagm)
        dma("sync", PQM, pqc)
        dma("sync", PKM, pkc)
        dma("sync", ONES, onesc)
        dma("sync", v_tails, tailc)
        vmemset(v_zero, 0.0)
        vmemset(v_eps, EPS)
        vmemset(v_1285, 128.5)
        tconst = dma("sync", b_sb, bffn)

        sq_ps = PS2[:, 0:1024]

        def layernorm(xt, dst, ready, rstd=None, biasv=None):
            v_sum = vec.slot()
            vv = [vec.slot() for _ in range(4)]
            rst = rstd if rstd is not None else vec.slot()
            bia = biasv if biasv is not None else vec.slot()
            tsum = vred(v_sum, xt, ALU.add, deps=ready, sig=True)
            tsq = None
            for i in range(4):
                tsq = act(sq_ps, xt[:, i * 1024:(i + 1) * 1024], ACTF.Square,
                          accum=vv[i], deps=ready, sig=(i == 3))
            c1 = vtt(vv[0], vv[0], vv[1], ALU.add, deps=[tsq], sig=True)
            c2 = vtt(vv[2], vv[2], vv[3], ALU.add, deps=[tsq], sig=True)
            c3 = vtt(vv[0], vv[0], vv[2], ALU.add, deps=[c1, c2], sig=True)
            c4 = vts(vv[0], vv[0], 1.0 / 4096.0, ALU.mult, deps=[c3],
                     sig=True)
            c5 = vts(v_sum, v_sum, 1.0 / 4096.0, ALU.mult, deps=[tsum],
                     sig=True)
            c6 = vtt(vv[1], v_sum, v_sum, ALU.mult, deps=[c5], sig=True)
            tvar = vtt(vv[0], vv[0], vv[1], ALU.subtract, deps=[c4, c6],
                       sig=True)
            tsd = act(rst, vv[0], ACTF.Sqrt, bias=v_eps, deps=[tvar],
                      sig=True)
            trst = vrecip(rst, rst, deps=[tsd], sig=True)
            tb = vstt(bia, v_sum, -1.0, rst, ALU.mult, ALU.mult,
                      deps=[trst, c5], sig=True)
            tap = act(dst, xt, ACTF.Identity, bias=bia, scale=rst, deps=[tb],
                      sig=True)
            return tap, rst, bia

        WBUF = [PW.ap()[:, 0:512], PW.ap()[:, 512:1024]]
        STAGE = PW.ap()[:, 1024:5120]
        T1A = PW.ap()[:, 5120:6144]
        TMP2A = PW.ap()[:, 6144:7168]
        XT = PX.ap()[:, 0:4096]
        CKVB = PX.ap()[:, 4096:8192]
        PH3_1024 = PH.ap().rearrange("p (c m) -> p c m", m=1024)
        PH3_512 = PH.ap().rearrange("p (c m) -> p c m", m=512)
        PH3_4096 = PH.ap().rearrange("p (c m) -> p c m", m=4096)

        def hcol(kc):
            return PH.ap()[:, kc * 1024:(kc + 1) * 1024]

        def phase_A(half):
            joinA = s.join()
            trope = [dma("sync", COSQ, cosq_in[half], deps=joinA),
                     dma("sync", SINQ, sinq_in[half]),
                     dma("sync", COSK, cosk_in[half]),
                     dma("sync", SINKD, sink_in[half]), tconst]

            h_done = []
            last_xuse = None
            for t in range(8):
                txl = dma("sync", XT, xh[half][t * 128:(t + 1) * 128, :],
                          deps=[last_xuse] if last_xuse else joinA)
                tap, _, _ = layernorm(XT, XT, [txl])
                for g4 in range(8):
                    ttp = None
                    for j in range(4):
                        kc = g4 * 4 + j
                        ttp = tp(PS2[:, 1024 + j * 128:1152 + j * 128],
                                 XT[:, kc * 128:(kc + 1) * 128],
                                 deps=([tap, h_done[-1]] if h_done else [tap]),
                                 sig=(j == 3))
                    dst = PH3_1024[:, g4 * 4:(g4 + 1) * 4,
                                   t * 128:(t + 1) * 128]
                    tev = vcopy(dst, PS2[:, 1024:1536].rearrange(
                        "p (a m) -> p a m", a=4), deps=[ttp], sig=True)
                    h_done.append(tev)
                last_xuse = h_done[-1]
            th = h_done[-1]

            def proj(wt_ap, Kdim, Mout, rhs_fn, sink_fn, wdeps,
                     init_sink=None):
                nks = Kdim // 128
                last_sink = init_sink
                wtok = [None, None]
                for ms in range(Mout // 512):
                    for ks in range(nks):
                        slot = ks % 2
                        piece = wt_ap[ks * 128:(ks + 1) * 128,
                                      ms * 512:(ms + 1) * 512]
                        d = list(wdeps)
                        if wtok[slot] is not None:
                            d.append(wtok[slot])
                        tw = dma("sync", WBUF[slot], piece, deps=d)
                        lastmm = None
                        for th_i in range(2):
                            psb = PS1 if th_i == 0 else PS2
                            for mi in range(4):
                                dd = [tw]
                                if ks == 0 and last_sink is not None:
                                    dd.append(last_sink)
                                lastmm = mm(
                                    psb[:, mi * 512:(mi + 1) * 512],
                                    WBUF[slot][:, mi * 128:(mi + 1) * 128],
                                    rhs_fn(ks, th_i),
                                    start=(ks == 0), stop=(ks == nks - 1),
                                    deps=dd,
                                    sig=(mi == 3 and th_i == 1))
                        wtok[slot] = lastmm
                    mmtok = wtok[(nks - 1) % 2]
                    tev = None
                    for th_i in range(2):
                        psb = PS1 if th_i == 0 else PS2
                        for mi in range(4):
                            tev = vcopy(
                                STAGE[:, mi * 1024 + th_i * 512:
                                      mi * 1024 + th_i * 512 + 512],
                                psb[:, mi * 512:(mi + 1) * 512],
                                deps=[mmtok, last_sink], sig=True)
                    last_sink = sink_fn(ms, tev)
                return last_sink

            def rhs_h(ks, th_i):
                return hcol(ks)[:, th_i * 512:(th_i + 1) * 512]

            def rhs_ckv(ks, th_i):
                return CKVB[:, ks * 1024 + th_i * 512:
                            ks * 1024 + th_i * 512 + 512]

            def sink_kr(ms, tev):
                tend = tev
                prev_tro = None
                for mi in range(4):
                    ch = STAGE[:, mi * 1024:(mi + 1) * 1024]
                    tsw = []
                    for th_i in range(2):
                        dd = [tev] + trope
                        if prev_tro is not None:
                            dd.append(prev_tro)
                        tsw.append(mm(
                            PS2[:, 1024 + th_i * 512:1536 + th_i * 512],
                            PKM, ch[:, th_i * 512:(th_i + 1) * 512],
                            start=True, stop=True, deps=dd, sig=True))
                    t1 = vtt(T1A, ch, COSK, ALU.mult, deps=[tev] + trope,
                             sig=True)
                    t2a = vtt(TMP2A[:, 0:512], PS2[:, 1024:1536],
                              SINKD[:, 0:512], ALU.mult, deps=[tsw[0]],
                              sig=True)
                    t2b = vtt(TMP2A[:, 512:1024], PS2[:, 1536:2048],
                              SINKD[:, 512:1024], ALU.mult, deps=[tsw[1]],
                              sig=True)
                    prev_tro = vtt(ch, T1A, TMP2A, ALU.add,
                                   deps=[t1, t2a, t2b], sig=True)
                    ka, kb = 2 * mi, 2 * mi + 1
                    for (r0, kv, dr) in ((0, ka, 0), (32, ka, 64),
                                         (64, kb, 0), (96, kb, 64)):
                        tend = dma("gpsimd",
                                   k_sc[kv, dr:dr + 32,
                                        half * HALF:(half + 1) * HALF],
                                   ch[r0:r0 + 32, :], deps=[prev_tro])
                return tend

            tkr = proj(wkrt, D, RD, rhs_h, sink_kr, [th])

            def sink_dk(ms, tev):
                t = None
                for mi in range(4):
                    t = vcopy(CKVB[:, mi * 1024:(mi + 1) * 1024],
                              STAGE[:, mi * 1024:(mi + 1) * 1024],
                              deps=[tev], sig=(mi == 3))
                return t

            tdk = proj(wdkt, D, CKV, rhs_h, sink_dk, [th, tkr],
                       init_sink=tkr)

            def sink_upk(ms, tev):
                tend = tev
                for mi in range(4):
                    ch = STAGE[:, mi * 1024:(mi + 1) * 1024]
                    ka, kb = 2 * mi, 2 * mi + 1
                    for (r0, kv, dr) in ((0, ka, 32), (32, ka, 96),
                                         (64, kb, 32), (96, kb, 96)):
                        tend = dma("gpsimd",
                                   k_sc[kv, dr:dr + 32,
                                        half * HALF:(half + 1) * HALF],
                                   ch[r0:r0 + 32, :], deps=[tev])
                return tend

            tupk = proj(wupkt, CKV, CKV, rhs_ckv, sink_upk, [tdk],
                        init_sink=tdk)

            def sink_upv(ms, tev):
                tend = tev
                ttp_prev = None
                for mi in range(4):
                    kv = ms * 4 + mi
                    ch = STAGE[:, mi * 1024:(mi + 1) * 1024]
                    for tb4 in range(2):
                        ttp = None
                        for j in range(4):
                            tb = tb4 * 4 + j
                            dd = [tev]
                            if ttp_prev is not None:
                                dd.append(ttp_prev)
                            ttp = tp(PS2[:, 1024 + j * 128:1152 + j * 128],
                                     ch[:, tb * 128:(tb + 1) * 128],
                                     deps=dd, sig=(j == 3))
                        tvv = vcopy(T1A[:, 0:512], PS2[:, 1024:1536],
                                    deps=[ttp, tend], sig=True)
                        ttp_prev = tvv
                        dst = v_sc[kv, half * HALF + tb4 * 512:
                                   half * HALF + (tb4 + 1) * 512, :]
                        tend = dma("gpsimd",
                                   dst.rearrange("(j p) d -> p j d", p=128),
                                   T1A[:, 0:512].rearrange(
                                       "p (j d) -> p j d", j=4),
                                   deps=[tvv])
                return tend

            tupv = proj(wupvt, CKV, NKV * HD, rhs_ckv, sink_upv,
                        [tupk], init_sink=tupk)

            def projq(wdeps):
                last_sink = None
                wtok = [None, None]
                for ms in range(8):
                    for ks in range(32):
                        slot = ks % 2
                        piece = wqt[ks * 128:(ks + 1) * 128,
                                    ms * 512:(ms + 1) * 512]
                        d = list(wdeps)
                        if wtok[slot] is not None:
                            d.append(wtok[slot])
                        tw = dma("sync", WBUF[slot], piece, deps=d)
                        lastmm = None
                        for mi in range(4):
                            dd = [tw]
                            if ks == 0 and last_sink is not None:
                                dd.append(last_sink)
                            lastmm = mm(PS1[:, mi * 512:(mi + 1) * 512],
                                        WBUF[slot][:, mi * 128:(mi + 1) * 128],
                                        hcol(ks)[:, 0:512],
                                        start=(ks == 0), stop=(ks == 31),
                                        deps=dd, sig=(mi == 3))
                        wtok[slot] = lastmm
                    tev = None
                    for mi in range(4):
                        tev = vcopy(STAGE[:, mi * 512:(mi + 1) * 512],
                                    PS1[:, mi * 512:(mi + 1) * 512],
                                    deps=[wtok[1], last_sink], sig=True)
                    tend = tev
                    prev_tro = None
                    for mi in range(4):
                        g = ms * 4 + mi
                        ch = STAGE[:, mi * 512:(mi + 1) * 512]
                        dd = [tev] + trope
                        if prev_tro is not None:
                            dd.append(prev_tro)
                        tsw = mm(PS2[:, 1024:1536], PQM, ch, start=True,
                                 stop=True, deps=dd, sig=True)
                        t1 = vtt(T1A[:, 0:512], ch, COSQ, ALU.mult,
                                 deps=[tev] + trope, sig=True)
                        t2 = vtt(TMP2A[:, 0:512], PS2[:, 1024:1536], SINQ,
                                 ALU.mult, deps=[tsw], sig=True)
                        prev_tro = vtt(ch, T1A[:, 0:512], TMP2A[:, 0:512],
                                       ALU.add, deps=[t1, t2], sig=True)
                        tend = dma("gpsimd",
                                   q_sc[g * 128:(g + 1) * 128,
                                        half * 512:(half + 1) * 512],
                                   ch, deps=[prev_tro])
                    last_sink = tend
                return last_sink

            return projq([th, tupv])

        KBUF = PX.ap()[:, 0:2048]
        VBUF = PX.ap()[:, 2048:4096]
        QBUF = PX.ap()[:, 4096:6144]
        PSB = PW.ap()[:, 0:2048]
        PTSB = [PW.ap()[:, 2048:2176], PW.ap()[:, 2176:2304]]
        AOTM = PW.ap()[:, 2304:2432]
        AOST = PW.ap()[:, 2432:2944]

        def phase_B(p):
            joinB = s.join()
            TAILP = v_tails[:, p:p + 1].broadcast_to([128, 512])
            Lmax = 1024 if p == 0 else 2048
            lastk = joinB
            prev_texp = None
            prev_tmx = None
            prev_tao = None
            prev_ttf = None
            prev_tev2 = None
            for kv in range(NKV):
                dk = lastk if isinstance(lastk, list) else [lastk]
                tk = dma("sync", KBUF[:, 0:Lmax], k_sc[kv, :, 0:Lmax], deps=dk)
                tv = dma("sync",
                         VBUF[:, 0:Lmax].rearrange("p (j d) -> p j d", d=128),
                         v_sc[kv, 0:Lmax, :].rearrange("(j p) d -> p j d",
                                                       p=128), deps=dk)
                tq = dma("sync",
                         QBUF.rearrange("p (h t) -> p h t", h=4),
                         q_sc[kv * 512:(kv + 1) * 512,
                              p * 512:(p + 1) * 512].rearrange(
                                  "(h p) t -> p h t", p=128), deps=dk)
                slot_last = []
                for sl in range(4):
                    sidx = p * 4 + sl
                    if p == 0:
                        part1 = list(range(sl + 1))
                        part2 = [4, 5, 6, 7]
                        diagpos = sl
                    else:
                        part1 = list(range(5 + sidx))
                        part2 = [12, 13, 14, 15]
                        diagpos = 4 + sidx
                    blocks = part1 + part2
                    nblk = len(blocks)
                    L = nblk * 128
                    head_last = []
                    for hh in range(4):
                        qt = QBUF[:, hh * 512 + sl * 128:
                                  hh * 512 + (sl + 1) * 128]
                        tsc = None
                        runs = []
                        off = 0
                        n1 = len(part1)
                        for c0 in range(0, n1, 4):
                            w = min(4, n1 - c0)
                            runs.append((off, part1[c0], w))
                            off += w
                        po2, b02, w2 = off, part2[0], 4
                        while w2 > 0:
                            w1 = min(w2, 4 - (po2 % 4))
                            runs.append((po2, b02, w1))
                            po2 += w1
                            b02 += w1
                            w2 -= w1
                        for (po, b0, w) in runs:
                            dd = [tk, tq]
                            if prev_texp is not None:
                                dd += [prev_texp, prev_tmx]
                            tsc = mm(PS1[:, po * 128:(po + w) * 128], qt,
                                     KBUF[:, b0 * 128:(b0 + w) * 128],
                                     start=True, stop=True, deps=dd,
                                     sig=(po + w == nblk))
                        tmk1 = vtt(PS1[:, diagpos * 128:(diagpos + 1) * 128],
                                   PS1[:, diagpos * 128:(diagpos + 1) * 128],
                                   DIAG, ALU.add, deps=[tsc], sig=True)
                        tmk2 = vtt(PS1[:, L - 512:L], PS1[:, L - 512:L],
                                   TAILP, ALU.add, deps=[tsc], sig=True)
                        prev_tmx = vred(v_m, PS1[:, 0:L], ALU.max,
                                        deps=[tsc, tmk1, tmk2], neg=True,
                                        sig=True)
                        tbb = vts(v_bi, v_m, INV_SQRT_HD, ALU.mult,
                                  deps=[prev_texp, prev_tmx], sig=True)
                        texp = act(PSB[:, 0:L], PS1[:, 0:L], ACTF.Exp,
                                   bias=v_bi, scale=INV_SQRT_HD, accum=v_sm,
                                   deps=[tbb, tsc, prev_tmx], sig=True)
                        prev_texp = texp
                        trc = vrecip(v_rc, v_sm, deps=[texp, prev_tao],
                                     sig=True)
                        tmmv = None
                        slot_busy = [None, None]
                        for j, blk in enumerate(blocks):
                            pslot = j % 2
                            dtp = [texp]
                            if slot_busy[pslot] is not None:
                                dtp.append(slot_busy[pslot])
                            ttp = tp(PS2[:, pslot * 128:(pslot + 1) * 128],
                                     PSB[:, j * 128:(j + 1) * 128],
                                     deps=dtp, sig=True)
                            tevp = vcopy(PTSB[pslot],
                                         PS2[:, pslot * 128:(pslot + 1) * 128],
                                         deps=[ttp], sig=True)
                            tmmv = mm(PS2[:, 512:640], PTSB[pslot],
                                      VBUF[:, blk * 128:(blk + 1) * 128],
                                      start=(j == 0), stop=(j == nblk - 1),
                                      deps=[tevp, tv], sig=(j == nblk - 1))
                            slot_busy[pslot] = tevp
                        tao = act(AOTM, PS2[:, 512:640], ACTF.Identity,
                                  scale=v_rc, deps=[tmmv, trc], sig=True)
                        prev_tao = tao
                        dd = [tao]
                        if prev_tev2 is not None:
                            dd.append(prev_tev2)
                        ttf = tp(PS2[:, 1024:1152], AOTM, deps=dd, sig=True)
                        prev_ttf = ttf
                        dd = [ttf]
                        if len(slot_last) >= 1:
                            dd.append(slot_last[-1])
                        tev2 = vcopy(AOST[:, hh * 128:(hh + 1) * 128],
                                     PS2[:, 1024:1152], deps=dd, sig=True)
                        prev_tev2 = tev2
                        head_last.append(tev2)
                    dst = ao_sc[kv * 512:(kv + 1) * 512,
                                sidx * 128:(sidx + 1) * 128]
                    tst = dma("gpsimd",
                              dst.rearrange("(h p) c -> p h c", p=128),
                              AOST.rearrange("p (h c) -> p h c", h=4),
                              deps=head_last)
                    slot_last.append(tst)
                lastk = slot_last[-1]
            return lastk

        def phase_C(half):
            joinC = s.join()
            tao = None
            for kc in range(32):
                tao = dma("sync", PH.ap()[:, kc * 512:(kc + 1) * 512],
                          ao_sc[kc * 128:(kc + 1) * 128,
                                half * 512:(half + 1) * 512],
                          deps=(joinC if kc == 0 else ()))
            YSTG = PW.ap()[:, 1024:3072]
            wtok = [None, None]
            last_tr = None
            prev_sink = None
            for ms in range(8):
                lastmm = None
                for ks in range(32):
                    slot = ks % 2
                    piece = wot[ks * 128:(ks + 1) * 128,
                                ms * 512:(ms + 1) * 512]
                    d = [tao]
                    if wtok[slot] is not None:
                        d.append(wtok[slot])
                    tw = dma("sync", WBUF[slot], piece, deps=d)
                    for mi in range(4):
                        dd = [tw]
                        if ks == 0 and last_tr is not None:
                            dd.append(last_tr)
                        lastmm = mm(PS1[:, mi * 512:(mi + 1) * 512],
                                    WBUF[slot][:, mi * 128:(mi + 1) * 128],
                                    PH.ap()[:, ks * 512:(ks + 1) * 512],
                                    start=(ks == 0), stop=(ks == 31),
                                    deps=dd, sig=(mi == 3))
                    wtok[slot] = lastmm
                tev = None
                for mi in range(4):
                    tev = vcopy(YSTG[:, mi * 512:(mi + 1) * 512],
                                PS1[:, mi * 512:(mi + 1) * 512],
                                deps=[lastmm, prev_sink], sig=(mi == 3))
                for tb in range(4):
                    ttp = None
                    for mi in range(4):
                        ttp = tp(PS2[:, 1024 + mi * 128:1152 + mi * 128],
                                 YSTG[:, mi * 512 + tb * 128:
                                      mi * 512 + (tb + 1) * 128],
                                 deps=([tev, last_tr] if last_tr else [tev]),
                                 sig=(mi == 3))
                    last_tr = vcopy(
                        PH3_4096[:, 4 + tb, ms * 512:(ms + 1) * 512],
                        PS2[:, 1024:1536], deps=[ttp], sig=True)
                prev_sink = last_tr
            tlast = None
            for tb in range(4):
                yt = PH3_4096[:, 4 + tb, :]
                txl = dma("sync", XT,
                          x_own[half * 512 + tb * 128:
                                half * 512 + (tb + 1) * 128, :],
                          deps=[tlast] if tlast else [last_tr])
                tap, _, _ = layernorm(yt, yt, [last_tr])
                tadd = vtt(yt, yt, XT, ALU.add, deps=[tap, txl], sig=True)
                tlast = dma("gpsimd",
                            x1_sc[half * 512 + tb * 128:
                                  half * 512 + (tb + 1) * 128, :],
                            yt, deps=[tadd])
            return tlast

        def phase_D0():
            joinD = s.join()
            tm = vmemset(v_acc, 0.0, deps=joinD, sig=True)
            WCH = PW.ap()[:, 1024:5120]
            va = V1[:, 505:506]
            t1 = tm
            tacc = tm
            for ks in range(32):
                tw = dma("sync", WCH, wft[ks * 128:(ks + 1) * 128, :],
                         deps=(joinD if ks == 0 else [t1]))
                t1 = vred(va, WCH, ALU.add, deps=[tw], absv=True, sig=True)
                tacc = vtt(v_acc, v_acc, va, ALU.add, deps=[t1, tacc],
                           sig=True)
            trep = mm(PS2[:, 0:1], ONES, v_acc, start=True, stop=True,
                      deps=[tacc], sig=True)
            tg0 = vts(v_gmc, PS2[:, 0:1], 1.0 / 16777216.0, ALU.mult,
                      deps=[trep], sig=True)
            tgm = vts(v_gmc, v_gmc, EPS, ALU.max, deps=[tg0], sig=True)
            tth = vts(v_th, v_gmc, 0.5, ALU.mult, deps=[tgm], sig=True)
            ttn = vts(v_thn, v_th, -1.0, ALU.mult, deps=[tth], sig=True)
            return [tgm, tth, ttn]

        def phase_D(half, tth):
            joinD = s.join()
            X1T = PX.ap()[:, 0:4096]
            FT = PX.ap()[:, 4096:8192]
            U0 = PH.ap()[:, 16384:20480]
            UU = PH.ap()[:, 20480:24576]
            GG = PH.ap()[:, 24576:28672]
            last_t = None
            for tb in range(4):
                tx1 = dma("sync", X1T,
                          x1_sc[half * 512 + tb * 128:
                                half * 512 + (tb + 1) * 128, :],
                          deps=(joinD if tb == 0 else [last_t]))
                tf, _, _ = layernorm(X1T, FT, [tx1],
                                     rstd=rstd3[half * 4 + tb],
                                     biasv=bias3[half * 4 + tb])
                tsq = None
                for i in range(4):
                    tsq = act(sq_ps, FT[:, i * 1024:(i + 1) * 1024],
                              ACTF.Square, accum=v_ssa[i], deps=[tf],
                              sig=(i == 3))
                d1 = vtt(v_ssa[0], v_ssa[0], v_ssa[1], ALU.add, deps=[tsq],
                         sig=True)
                d2 = vtt(v_ssa[2], v_ssa[2], v_ssa[3], ALU.add, deps=[tsq],
                         sig=True)
                d3 = vtt(v_ssa[0], v_ssa[0], v_ssa[2], ALU.add,
                         deps=[d1, d2], sig=True)
                tss = vts(v_ssa[0], v_ssa[0], 1.0 / 4096.0, ALU.mult,
                          deps=[d3], sig=True)
                tr3 = act(v_r3, v_ssa[0], ACTF.Sqrt, bias=v_eps, deps=[tss],
                          sig=True)
                tr3 = vrecip(v_r3, v_r3, deps=[tr3], sig=True)
                tam = vred(v_am, FT, ALU.max, deps=[tf], absv=True, sig=True)
                d4 = vtt(v_am, v_am, v_r3, ALU.mult, deps=[tam, tr3],
                         sig=True)
                tc = vts(v_am, v_am, EPS, ALU.max, deps=[d4], sig=True)
                vts(sinv3[half * 4 + tb], v_am, 1.0 / 127.0, ALU.mult,
                    deps=[tc], sig=True)
                trcd = vrecip(v_rcd, v_am, deps=[tc], sig=True)
                tsv = vstt(svec3[half * 4 + tb], v_rcd, 127.0, v_r3,
                           ALU.mult, ALU.mult, deps=[trcd, tr3], sig=True)
                tu0 = act(U0, FT, ACTF.Identity, bias=v_1285,
                          scale=svec3[half * 4 + tb],
                          deps=[tsv, last_t], sig=True)
                tu = vcopy(UU, U0, deps=[tu0], sig=True)
                for bit in (128.0, 64.0, 32.0, 16.0, 8.0, 4.0, 2.0, 1.0):
                    tg = vts(GG, UU, bit, ALU.is_ge, deps=[tu], sig=True)
                    tu = vstt(UU, GG, -bit, UU, ALU.mult, ALU.add, deps=[tg],
                              sig=True)
                trr = vstt(GG, U0, -128.0, UU, ALU.add, ALU.subtract,
                           deps=[tu, tu0], sig=True)
                txq = act(GG, GG, ACTF.Identity, scale=sinv3[half * 4 + tb],
                          deps=[trr], sig=True)
                for g4 in range(8):
                    ttp = None
                    for j in range(4):
                        kc = g4 * 4 + j
                        ttp = tp(PS2[:, 1024 + j * 128:1152 + j * 128],
                                 GG[:, kc * 128:(kc + 1) * 128],
                                 deps=([txq, last_t] if last_t else [txq]),
                                 sig=(j == 3))
                    dst = PH3_512[:, g4 * 4:(g4 + 1) * 4,
                                  tb * 128:(tb + 1) * 128]
                    last_t = vcopy(dst, PS2[:, 1024:1536].rearrange(
                        "p (a m) -> p a m", a=4), deps=[ttp], sig=True)
            txq_done = last_t

            TSTG = PW.ap()[:, 1024:2048]
            G1 = PW.ap()[:, 2048:2560]
            YSTG2 = PW.ap()[:, 2560:3072]
            wtok = [None, None]
            ttr = [None, None]
            last_tr = None
            prev_ttp4 = None
            for ms in range(8):
                lastmm = None
                for ks in range(32):
                    slot = ks % 2
                    piece = wft[ks * 128:(ks + 1) * 128,
                                ms * 512:(ms + 1) * 512]
                    d = [txq_done] + tth
                    if ttr[slot] is not None:
                        d.append(ttr[slot])
                    tw = dma("sync", WBUF[slot], piece, deps=d)
                    tt = TSTG[:, slot * 512:(slot + 1) * 512]
                    dtern = [tw] + tth
                    if wtok[slot] is not None:
                        dtern.append(wtok[slot])
                    tg1 = vts(G1, WBUF[slot], v_th, ALU.is_ge, deps=dtern,
                              sig=True)
                    tg2 = vts(tt, WBUF[slot], v_thn, ALU.is_le, deps=[tw],
                              sig=True)
                    tg3 = vtt(tt, G1, tt, ALU.subtract, deps=[tg1, tg2],
                              sig=True)
                    ttr[slot] = tg3
                    for mi in range(4):
                        dd = [tg3]
                        if ks == 0 and last_tr is not None:
                            dd.append(last_tr)
                        lastmm = mm(PS1[:, mi * 512:(mi + 1) * 512],
                                    tt[:, mi * 128:(mi + 1) * 128],
                                    PH.ap()[:, ks * 512:(ks + 1) * 512],
                                    start=(ks == 0), stop=(ks == 31),
                                    deps=dd, sig=(mi == 3))
                    wtok[slot] = lastmm
                for mi in range(4):
                    mg = ms * 4 + mi
                    dd = [lastmm]
                    if prev_ttp4 is not None:
                        dd.append(prev_ttp4)
                    tge = act(YSTG2, PS1[:, mi * 512:(mi + 1) * 512],
                              ACTF.Gelu, bias=b_sb[:, mg:mg + 1],
                              scale=v_gmc, deps=dd, sig=True)
                    ttp = None
                    for tb in range(4):
                        ddt = [tge]
                        if last_tr is not None:
                            ddt.append(last_tr)
                        ttp = tp(PS2[:, 1024 + tb * 128:1152 + tb * 128],
                                 YSTG2[:, tb * 128:(tb + 1) * 128],
                                 deps=ddt, sig=(tb == 3))
                    prev_ttp4 = ttp
                    dst = PH3_4096[:, 4 + 0:4 + 4, mg * 128:(mg + 1) * 128]
                    last_tr = vcopy(dst, PS2[:, 1024:1536].rearrange(
                        "p (a m) -> p a m", a=4), deps=[ttp], sig=True)
            tlast = None
            for tb in range(4):
                gt = PH3_4096[:, 4 + tb, :]
                tx1 = dma("sync", X1T,
                          x1_sc[half * 512 + tb * 128:
                                half * 512 + (tb + 1) * 128, :],
                          deps=([last_tr] if tb == 0 else [tlast]))
                tfa = act(FT, X1T, ACTF.Identity, bias=bias3[half * 4 + tb],
                          scale=rstd3[half * 4 + tb], deps=[tx1], sig=True)
                tsum = vtt(FT, FT, gt, ALU.add, deps=[tfa, last_tr], sig=True)
                tap, _, _ = layernorm(FT, FT, [tsum])
                tout = vtt(FT, FT, X1T, ALU.add, deps=[tap], sig=True)
                tlast = dma("gpsimd",
                            out[half * 512 + tb * 128:
                                half * 512 + (tb + 1) * 128, :],
                            FT, deps=[tout])
            return tlast

        phase_A(0)
        phase_B(0)
        phase_A(1)
        phase_B(1)
        phase_C(0)
        phase_C(1)
        tth = phase_D0()
        phase_D(0, tth)
        phase_D(1, tth)

        s.op("vector", (lambda e: e.engine_nop()),
             deps=[("G", s.cnt["G"])])

        with nc.Block() as blk:
            def make(engname):
                def run(eng):
                    for wl, emit, tok in s.stream[engname]:
                        for (sm, v) in wl:
                            eng.wait_ge(sem[sm], v)
                        inst = emit(eng)
                        if tok is not None:
                            inst.then_inc(sem[tok[0]],
                                          16 if tok[0] in ("W", "G") else 1)
                return run

            blk.sync(make("sync"))
            blk.gpsimd(make("gpsimd"))
            blk.tensor(make("tensor"))
            blk.vector(make("vector"))
            blk.scalar(make("scalar"))

    return nc


def _host_tables():
    HEAD_DIM = 128
    ROPE_THETA = 10000.0
    inv_freq = 1.0 / ROPE_THETA ** (np.arange(0, HEAD_DIM, 2,
                                              dtype=np.float32) / HEAD_DIM)
    pos = np.arange(S, dtype=np.float32)
    freqs = np.outer(pos, inv_freq)
    emb = np.concatenate([freqs, freqs], axis=-1)
    cos = np.cos(emb).astype(np.float32)
    sin = np.sin(emb).astype(np.float32)
    return cos, sin


def kernel(L0_attn_hidden_states, L0_attn_Wq, L0_attn_Wkr, L0_attn_Wdk,
           L0_attn_Wupk, L0_attn_Wupv, L0_attn_Wo, L0_ffn_W, L0_ffn_b):
    X = np.ascontiguousarray(np.asarray(L0_attn_hidden_states, np.float32))
    WqT = np.ascontiguousarray(np.asarray(L0_attn_Wq, np.float32).T)
    WkrT = np.ascontiguousarray(np.asarray(L0_attn_Wkr, np.float32).T)
    WdkT = np.ascontiguousarray(np.asarray(L0_attn_Wdk, np.float32).T)
    WupkT = np.ascontiguousarray(np.asarray(L0_attn_Wupk, np.float32).T)
    WupvT = np.ascontiguousarray(np.asarray(L0_attn_Wupv, np.float32).T)
    WoT = np.ascontiguousarray(np.asarray(L0_attn_Wo, np.float32).T)
    WfT = np.ascontiguousarray(np.asarray(L0_ffn_W, np.float32).T)
    bf = np.ascontiguousarray(
        np.asarray(L0_ffn_b, np.float32).reshape(32, 128).T)

    cos, sin = _host_tables()
    ident = np.eye(128, dtype=np.float32)
    ones = np.ones((128, 128), dtype=np.float32)
    diag = np.where(np.arange(128)[None, :] <= np.arange(128)[:, None],
                    np.float32(0.0), np.float32(NEG1E9)).astype(np.float32)
    pqm = np.zeros((128, 128), np.float32)
    for i in range(64):
        pqm[i, 64 + i] = 1.0
        pqm[64 + i, i] = 1.0
    pkm = np.zeros((128, 128), np.float32)
    for base in (0, 64):
        for i in range(32):
            pkm[base + i, base + 32 + i] = 1.0
            pkm[base + 32 + i, base + i] = 1.0

    cosq_rows = np.ascontiguousarray(cos.T)
    sinq_rows = sin.T.copy()
    sinq_rows[0:64] *= -1.0
    kmap = np.zeros(128, dtype=np.int64)
    for r in range(128):
        rr = r % 64
        kmap[r] = rr if rr < 32 else 64 + (rr - 32)
    cosk_rows = np.ascontiguousarray(cos.T[kmap])
    sink_rows = sin.T[kmap].copy()
    for r in range(128):
        if (r % 64) < 32:
            sink_rows[r] *= -1.0

    nc = build_program()
    in_maps = []
    own_blocks_all = []
    for c in range(8):
        b = c // 2
        odd = c % 2
        h0 = [4, 5, 6, 7, 0, 1, 2, 3] if odd else [0, 1, 2, 3, 4, 5, 6, 7]
        h1 = ([8, 9, 10, 11, 12, 13, 14, 15] if odd
              else [12, 13, 14, 15, 8, 9, 10, 11])
        own_blocks_all.append(h0[:4] + h1[:4])
        perm0 = np.concatenate([np.arange(blk * 128, (blk + 1) * 128)
                                for blk in h0])
        perm1 = np.concatenate([np.arange(blk * 128, (blk + 1) * 128)
                                for blk in h1])
        xb = X[b]
        tail = np.zeros((128, 2), np.float32)
        tail[:, 0 if not odd else 1] = np.float32(NEG1E9)
        m = {
            "x_h0": np.ascontiguousarray(xb[perm0]),
            "x_h1": np.ascontiguousarray(xb[perm1]),
            "x_own": np.ascontiguousarray(
                np.concatenate([xb[perm0[:512]], xb[perm1[:512]]])),
            "wqt": WqT, "wkrt": WkrT, "wdkt": WdkT, "wupkt": WupkT,
            "wupvt": WupvT, "wot": WoT, "wft": WfT, "bffn": bf,
            "identc": ident, "diagm": diag, "pq": pqm, "pk": pkm,
            "onesc": ones, "tailmask": tail,
            "cosq0": np.ascontiguousarray(cosq_rows[:, perm0[:512]]),
            "sinq0": np.ascontiguousarray(sinq_rows[:, perm0[:512]]),
            "cosq1": np.ascontiguousarray(cosq_rows[:, perm1[:512]]),
            "sinq1": np.ascontiguousarray(sinq_rows[:, perm1[:512]]),
            "cosk0": np.ascontiguousarray(cosk_rows[:, perm0]),
            "sink0": np.ascontiguousarray(sink_rows[:, perm0]),
            "cosk1": np.ascontiguousarray(cosk_rows[:, perm1]),
            "sink1": np.ascontiguousarray(sink_rows[:, perm1]),
        }
        in_maps.append(m)

    res = run_bass_kernel_spmd(nc, in_maps, core_ids=list(range(8)))
    full = np.empty((B, S, D), np.float32)
    for c in range(8):
        b = c // 2
        o = np.asarray(res.results[c]["out"]).reshape(8, 128, D)
        for i, blk in enumerate(own_blocks_all[c]):
            full[b, blk * 128:(blk + 1) * 128, :] = o[i]
    return full


# version 04
# speedup vs baseline: 1.0367x; 1.0367x over previous
import numpy as np
from contextlib import ExitStack

import concourse.bass as bass
import concourse.mybir as mybir
from concourse.bass_utils import run_bass_kernel_spmd

F32 = mybir.dt.float32
BF16 = mybir.dt.bfloat16
AXX = mybir.AxisListType.X
ALU = mybir.AluOpType
ACTF = mybir.ActivationFunctionType

B, S, D = 4, 2048, 4096
HALF = 1024
NKV = 8
HD = 128
CKV = 512
RD = 512
INV_SQRT_HD = 0.08838834764831845
NEG1E9 = -1.0e9
EPS = 1.0e-5

ENGINES = ("sync", "gpsimd", "tensor", "vector", "scalar")


DMA_CHANNELS = ("W0", "W1", "X0", "X1", "BM", "CT", "S0", "S1")


class Sched:

    def __init__(self):
        self.cnt = {"T": 0, "V": 0, "A": 0}
        for c in DMA_CHANNELS:
            self.cnt[c] = 0
        self.stream = {e: [] for e in ENGINES}
        self.lastw = {e: {} for e in ENGINES}
        self.own = {"tensor": "T", "vector": "V", "scalar": "A"}

    def op(self, eng, emit, deps=(), sig=False, ch=None):
        waits = {}
        for d in deps:
            if d is None:
                continue
            sm, v = d
            if v <= 0:
                continue
            if sm == "T" and self.own.get(eng) == "T":
                continue
            if v > waits.get(sm, 0):
                waits[sm] = v
        wl = []
        for sm, v in sorted(waits.items()):
            if self.lastw[eng].get(sm, 0) >= v:
                continue
            self.lastw[eng][sm] = v
            wl.append((sm, v))
        tok = None
        if sig:
            sm = ch if ch is not None else self.own[eng]
            amt = 16 if ch is not None else 1
            self.cnt[sm] += amt
            tok = (sm, self.cnt[sm])
        self.stream[eng].append([wl, emit, tok])
        return tok

    def retro_sig(self, eng):
        st = self.stream[eng]
        if not st:
            return None
        if st[-1][2] is None:
            sm = self.own[eng]
            self.cnt[sm] += 1
            st[-1][2] = (sm, self.cnt[sm])
        return st[-1][2]

    def join(self):
        toks = [t for t in (self.retro_sig(e)
                            for e in ("tensor", "vector", "scalar")) if t]
        for c in DMA_CHANNELS:
            if self.cnt[c] > 0:
                toks.append((c, self.cnt[c]))
        return toks


def build_program():
    nc = bass.Bass("TRN2", target_bir_lowering=False)

    def din(name, shape):
        return nc.dram_tensor(name, list(shape), F32, kind="ExternalInput").ap()

    xh = [din("x_h0", (HALF, D)), din("x_h1", (HALF, D))]
    x_own = din("x_own", (HALF, D))
    wqt = din("wqt", (D, D))
    wkrt = din("wkrt", (D, RD))
    wdkt = din("wdkt", (D, CKV))
    wupkt = din("wupkt", (CKV, CKV))
    wupvt = din("wupvt", (CKV, NKV * HD))
    wot = din("wot", (D, D))
    wft = din("wft", (D, D))
    bfull = din("bfull", (128, D))
    identc = din("identc", (128, 128))
    diagm = din("diagm", (128, 128))
    pqc = din("pq", (128, 128))
    pkc = din("pk", (128, 128))
    onesc = din("onesc", (128, 128))
    tailc = din("tailmask", (128, 2))
    cosq_in = [din("cosq0", (128, 512)), din("cosq1", (128, 512))]
    sinq_in = [din("sinq0", (128, 512)), din("sinq1", (128, 512))]
    cosk_in = [din("cosk0", (128, HALF)), din("cosk1", (128, HALF))]
    sink_in = [din("sink0", (128, HALF)), din("sink1", (128, HALF))]

    out = nc.dram_tensor("out", [HALF, D], F32, kind="ExternalOutput").ap()

    q_sc = nc.dram_tensor("q_sc", [D, HALF], F32, kind="Internal").ap()
    k_sc = nc.dram_tensor("k_sc", [NKV, 128, S], F32, kind="Internal").ap()
    v_sc = nc.dram_tensor("v_sc", [NKV, S, HD], F32, kind="Internal").ap()
    ao_sc = nc.dram_tensor("ao_sc", [D, HALF], F32, kind="Internal").ap()
    x1_sc = nc.dram_tensor("x1_sc", [HALF, D], F32, kind="Internal").ap()

    with ExitStack() as ctx:
        def sbuf(name, shape, dtype=F32):
            return ctx.enter_context(nc.sbuf_tensor(name, shape, dtype))

        PH = sbuf("PH", [128, 32768])
        PW = sbuf("PW", [128, 6144])
        PX = sbuf("PX", [128, 8192])
        PC = sbuf("PC", [128, 3712])
        PV = sbuf("PV", [128, 512])
        PB = sbuf("PB", [128, 3072], BF16)
        sem = {x: ctx.enter_context(nc.semaphore("sem_" + x))
               for x in (("T", "V", "A") + DMA_CHANNELS)}
        PS1 = ctx.enter_context(nc.psum_tensor("PS1", [128, 2048], F32))
        PS2 = ctx.enter_context(nc.psum_tensor("PS2", [128, 2048], F32))

        IDENT = PC.ap()[:, 0:128]
        DIAG = PC.ap()[:, 128:256]
        PQM = PC.ap()[:, 256:384]
        PKM = PC.ap()[:, 384:512]
        ONES = PC.ap()[:, 512:640]
        COSQ = PC.ap()[:, 640:1152]
        SINQ = PC.ap()[:, 1152:1664]
        COSK = PC.ap()[:, 1664:2688]
        SINKD = PC.ap()[:, 2688:3712]

        s = Sched()
        V1 = PV.ap()

        class Vec:
            def __init__(self):
                self.n = 0

            def slot(self, w=1):
                a = self.n
                self.n += w
                assert self.n <= 490, "vec pool overflow"
                return V1[:, a:a + w]

        vec = Vec()
        v_zero = vec.slot()
        v_tails = vec.slot(2)
        v_eps = vec.slot()
        v_1285 = vec.slot()
        v_acc = vec.slot()
        v_gmc = vec.slot()
        v_th = vec.slot()
        v_thn = vec.slot()
        rstd3 = [vec.slot() for _ in range(8)]
        bias3 = [vec.slot() for _ in range(8)]
        svg3 = [vec.slot() for _ in range(8)]
        v_m = V1[:, 494:495]
        v_sm = V1[:, 495:496]
        v_bi = V1[:, 496:497]
        v_rc = V1[:, 497:498]
        v_r3 = V1[:, 498:499]
        v_am = V1[:, 499:500]
        v_rcd = V1[:, 500:501]
        v_ssa = [V1[:, 501 + i:502 + i] for i in range(4)]
        va_d0 = V1[:, 505:506]

        st_rr = [0]

        def dma(queue, dst, src, deps=(), ch="CT"):
            if ch == "ST":
                ch = ("S0", "S1")[st_rr[0] % 2]
                st_rr[0] += 1
            deps = list(deps) + [(ch, s.cnt[ch])]
            return s.op(queue,
                        (lambda e, d=dst, so=src: e.dma_start(out=d, in_=so)),
                        deps, sig=True, ch=ch)

        def mm(outap, lhsT, rhs, start, stop, deps=(), sig=False):
            return s.op("tensor",
                        (lambda e, o=outap, l=lhsT, r=rhs, st=start, sp=stop:
                         e.matmul(o, l, r, start=st, stop=sp)), deps, sig=sig)

        def tp(outap, inap, deps=(), sig=False):
            return s.op("tensor",
                        (lambda e, o=outap, i=inap: e.transpose(o, i, IDENT)),
                        deps, sig=sig)

        def vcopy(dst, src, deps=(), sig=False):
            return s.op("vector",
                        (lambda e, d=dst, so=src: e.tensor_copy(d, so)),
                        deps, sig=sig)

        def vtt(dst, a, bb, op, deps=(), sig=False):
            return s.op("vector",
                        (lambda e, d=dst, x=a, y=bb, o=op:
                         e.tensor_tensor(d, x, y, op=o)), deps, sig=sig)

        def vts(dst, a, s1, op, deps=(), sig=False):
            return s.op("vector",
                        (lambda e, d=dst, x=a, c=s1, o=op:
                         e.tensor_single_scalar(d, x, c, op=o)), deps, sig=sig)

        def vstt(dst, in0, sc, in1, op0, op1, deps=(), sig=False):
            return s.op("vector",
                        (lambda e, d=dst, a=in0, c=sc, bb=in1, o0=op0, o1=op1:
                         e.scalar_tensor_tensor(d, a, c, bb, op0=o0, op1=o1)),
                        deps, sig=sig)

        def vred(dst, src, op, deps=(), sig=False, absv=False, neg=False):
            return s.op("vector",
                        (lambda e, d=dst, so=src, o=op, ab=absv, ng=neg:
                         e.tensor_reduce(d, so, axis=AXX, op=o,
                                         apply_absolute_value=ab, negate=ng)),
                        deps, sig=sig)

        def vmemset(dst, val, deps=(), sig=False):
            return s.op("vector", (lambda e, d=dst, v=val: e.memset(d, v)),
                        deps, sig=sig)

        def vrecip(dst, src, deps=(), sig=False):
            return s.op("vector",
                        (lambda e, d=dst, so=src: e.reciprocal(d, so)),
                        deps, sig=sig)

        def act(dst, src, func, bias=None, scale=1.0, accum=None, deps=(),
                sig=False):
            bias_ap = v_zero if bias is None else bias

            def emitf(e, d=dst, so=src, f=func, bb=bias_ap, sc=scale, au=accum):
                if au is None:
                    return e.activation(d, so, f, bias=bb, scale=sc)
                return e.activation(d, so, f, bias=bb, scale=sc, accum_out=au)
            return s.op("scalar", emitf, deps, sig=sig)

        dma("sync", IDENT, identc)
        dma("sync", DIAG, diagm)
        dma("sync", PQM, pqc)
        dma("sync", PKM, pkc)
        dma("sync", ONES, onesc)
        dma("sync", v_tails, tailc)
        vmemset(v_zero, 0.0)
        vmemset(v_eps, EPS)
        vmemset(v_1285, 128.5)
        tconst = s.join()

        sq_ps = PS2[:, 0:1024]

        def layernorm(xt, dst, ready, rstd=None, biasv=None,
                      sq_region=None):
            v_sum = vec.slot()
            vv = [vec.slot() for _ in range(4)]
            rst = rstd if rstd is not None else vec.slot()
            bia = biasv if biasv is not None else vec.slot()
            sq = sq_ps if sq_region is None else sq_region
            tsum = vred(v_sum, xt, ALU.add, deps=ready, sig=True)
            tsq = None
            for i in range(4):
                tsq = act(sq, xt[:, i * 1024:(i + 1) * 1024], ACTF.Square,
                          accum=vv[i], deps=ready, sig=(i == 3))
            c1 = vtt(vv[0], vv[0], vv[1], ALU.add, deps=[tsq], sig=True)
            c2 = vtt(vv[2], vv[2], vv[3], ALU.add, deps=[tsq], sig=True)
            c3 = vtt(vv[0], vv[0], vv[2], ALU.add, deps=[c1, c2], sig=True)
            c4 = vts(vv[0], vv[0], 1.0 / 4096.0, ALU.mult, deps=[c3],
                     sig=True)
            c5 = vts(v_sum, v_sum, 1.0 / 4096.0, ALU.mult, deps=[tsum],
                     sig=True)
            c6 = vtt(vv[1], v_sum, v_sum, ALU.mult, deps=[c5], sig=True)
            tvar = vtt(vv[0], vv[0], vv[1], ALU.subtract, deps=[c4, c6],
                       sig=True)
            tsd = act(rst, vv[0], ACTF.Sqrt, bias=v_eps, deps=[tvar],
                      sig=True)
            trst = vrecip(rst, rst, deps=[tsd], sig=True)
            tb = vstt(bia, v_sum, -1.0, rst, ALU.mult, ALU.mult,
                      deps=[trst, c5], sig=True)
            tap = act(dst, xt, ACTF.Identity, bias=bia, scale=rst, deps=[tb],
                      sig=True)
            return tap, rst, bia

        WBUF = [PW.ap()[:, 0:512], PW.ap()[:, 512:1024]]
        STAGE = [PW.ap()[:, 1024:5120], PX.ap()[:, 0:4096]]
        T1A = PW.ap()[:, 5120:5632]
        TMP2A = PW.ap()[:, 5632:6144]
        XTbuf = [PX.ap()[:, 0:4096], PX.ap()[:, 4096:8192]]
        CKVB = PX.ap()[:, 4096:8192]
        PH3_1024 = PH.ap().rearrange("p (c m) -> p c m", m=1024)
        PH3_4096 = PH.ap().rearrange("p (c m) -> p c m", m=4096)
        TSTB = [PB.ap()[:, 0:512], PB.ap()[:, 512:1024]]
        XQB = [PB.ap()[:, 1024:1536], PB.ap()[:, 1536:2048]]

        def hcol(kc):
            return PH.ap()[:, kc * 1024:(kc + 1) * 1024]

        def phase_A(half):
            joinA = s.join()
            trope = [dma("sync", COSQ, cosq_in[half], deps=joinA),
                     dma("sync", SINQ, sinq_in[half]),
                     dma("sync", COSK, cosk_in[half]),
                     dma("sync", SINKD, sink_in[half])] + tconst

            h_done = []
            tile_tp = [None, None]
            for t in range(8):
                xt = XTbuf[t % 2]
                d = list(joinA)
                if tile_tp[t % 2] is not None:
                    d.append(tile_tp[t % 2])
                txl = dma("sync", xt, xh[half][t * 128:(t + 1) * 128, :],
                          deps=d, ch=("X0", "X1")[t % 2])
                tap, _, _ = layernorm(xt, xt, [txl])
                for g4 in range(8):
                    ttp = None
                    for j in range(4):
                        kc = g4 * 4 + j
                        ttp = tp(PS2[:, 1024 + j * 128:1152 + j * 128],
                                 xt[:, kc * 128:(kc + 1) * 128],
                                 deps=([tap, h_done[-1]] if h_done else [tap]),
                                 sig=(j == 3))
                    dst = PH3_1024[:, g4 * 4:(g4 + 1) * 4,
                                   t * 128:(t + 1) * 128]
                    tev = vcopy(dst, PS2[:, 1024:1536].rearrange(
                        "p (a m) -> p a m", a=4), deps=[ttp], sig=True)
                    h_done.append(tev)
                tile_tp[t % 2] = ttp
            th = h_done[-1]

            def proj(wt_ap, Kdim, Mout, rhs_fn, sink_fn, wdeps,
                     init_sink=None):
                nks = Kdim // 128
                wtok = [None, None]
                ev1_prev, ev2_prev = None, None
                ps2rel_prev = None
                sink_tok = [init_sink, init_sink]
                last_sink = init_sink
                for ms in range(Mout // 512):
                    stg = STAGE[ms % 2]
                    for ks in range(nks):
                        slot = ks % 2
                        piece = wt_ap[ks * 128:(ks + 1) * 128,
                                      ms * 512:(ms + 1) * 512]
                        d = list(wdeps)
                        if wtok[slot] is not None:
                            d.append(wtok[slot])
                        tw = dma("sync", WBUF[slot], piece, deps=d,
                                 ch=("W0", "W1")[slot])
                        lastmm = None
                        for th_i in range(2):
                            psb = PS1 if th_i == 0 else PS2
                            for mi in range(4):
                                dd = [tw]
                                if ks == 0:
                                    dd.append(ev1_prev if th_i == 0
                                              else ev2_prev)
                                    if th_i == 1:
                                        dd.append(ps2rel_prev)
                                lastmm = mm(
                                    psb[:, mi * 512:(mi + 1) * 512],
                                    WBUF[slot][:, mi * 128:(mi + 1) * 128],
                                    rhs_fn(ks, th_i),
                                    start=(ks == 0), stop=(ks == nks - 1),
                                    deps=dd,
                                    sig=(mi == 3 and th_i == 1))
                        wtok[slot] = lastmm
                    mmtok = wtok[(nks - 1) % 2]
                    sdep = sink_tok[ms % 2]
                    ev1 = None
                    for half2 in range(2):
                        ev1 = vcopy(
                            stg.rearrange("p (m t) -> p m t", t=1024)
                            [:, half2 * 2:half2 * 2 + 2, 0:512],
                            PS1[:, half2 * 1024:(half2 + 1) * 1024]
                            .rearrange("p (m t) -> p m t", t=512),
                            deps=[mmtok, sdep], sig=True)
                    ev2 = None
                    for half2 in range(2):
                        ev2 = vcopy(
                            stg.rearrange("p (m t) -> p m t", t=1024)
                            [:, half2 * 2:half2 * 2 + 2, 512:1024],
                            PS2[:, half2 * 1024:(half2 + 1) * 1024]
                            .rearrange("p (m t) -> p m t", t=512),
                            deps=[mmtok, sdep], sig=True)
                    ev1_prev, ev2_prev = ev1, ev2
                    last_sink, ps2rel_prev = sink_fn(ms, ev2, stg,
                                                     ps2rel_prev)
                    sink_tok[ms % 2] = last_sink
                return last_sink

            def rhs_h(ks, th_i):
                return hcol(ks)[:, th_i * 512:(th_i + 1) * 512]

            def rhs_ckv(ks, th_i):
                return CKVB[:, ks * 1024 + th_i * 512:
                            ks * 1024 + th_i * 512 + 512]

            def sink_kr(ms, tev, stg, ps2rel):
                tend = tev
                prev_tro = [ps2rel, ps2rel]
                for mi in range(4):
                    ch = stg[:, mi * 1024:(mi + 1) * 1024]
                    tros = []
                    for th_i in range(2):
                        chh = ch[:, th_i * 512:(th_i + 1) * 512]
                        dd = [tev] + trope
                        if prev_tro[th_i] is not None:
                            dd.append(prev_tro[th_i])
                        tsw = mm(PS2[:, 1024 + th_i * 512:1536 + th_i * 512],
                                 PKM, chh, start=True, stop=True, deps=dd,
                                 sig=True)
                        t1 = vtt(T1A, chh,
                                 COSK[:, th_i * 512:(th_i + 1) * 512],
                                 ALU.mult, deps=[tev] + trope, sig=True)
                        t2 = vtt(TMP2A,
                                 PS2[:, 1024 + th_i * 512:1536 + th_i * 512],
                                 SINKD[:, th_i * 512:(th_i + 1) * 512],
                                 ALU.mult, deps=[tsw], sig=True)
                        tro = vtt(chh, T1A, TMP2A, ALU.add, deps=[t1, t2],
                                  sig=True)
                        prev_tro[th_i] = tro
                        tros.append(tro)
                    ka, kb = 2 * mi, 2 * mi + 1
                    for (r0, kv, dr) in ((0, ka, 0), (32, ka, 64),
                                         (64, kb, 0), (96, kb, 64)):
                        tend = dma("gpsimd",
                                   k_sc[kv, dr:dr + 32,
                                        half * HALF:(half + 1) * HALF],
                                   ch[r0:r0 + 32, :], deps=tros, ch="ST")
                return tend, prev_tro[1]

            tkr = proj(wkrt, D, RD, rhs_h, sink_kr, [th])

            def sink_dk(ms, tev, stg, ps2rel):
                t = None
                for mi in range(4):
                    t = vcopy(CKVB[:, mi * 1024:(mi + 1) * 1024],
                              stg[:, mi * 1024:(mi + 1) * 1024],
                              deps=[tev], sig=(mi == 3))
                return t, ps2rel

            tdk = proj(wdkt, D, CKV, rhs_h, sink_dk, [th, tkr],
                       init_sink=tkr)

            def sink_upk(ms, tev, stg, ps2rel):
                tend = tev
                for mi in range(4):
                    ch = stg[:, mi * 1024:(mi + 1) * 1024]
                    ka, kb = 2 * mi, 2 * mi + 1
                    for (r0, kv, dr) in ((0, ka, 32), (32, ka, 96),
                                         (64, kb, 32), (96, kb, 96)):
                        tend = dma("gpsimd",
                                   k_sc[kv, dr:dr + 32,
                                        half * HALF:(half + 1) * HALF],
                                   ch[r0:r0 + 32, :], deps=[tev], ch="ST")
                return tend, ps2rel

            tupk = proj(wupkt, CKV, CKV, rhs_ckv, sink_upk, [tdk],
                        init_sink=tdk)

            def sink_upv(ms, tev, stg, ps2rel):
                tend = tev
                ttp_prev = ps2rel
                for mi in range(4):
                    kv = ms * 4 + mi
                    ch = stg[:, mi * 1024:(mi + 1) * 1024]
                    for tb4 in range(2):
                        ttp = None
                        for j in range(4):
                            tb = tb4 * 4 + j
                            dd = [tev]
                            if ttp_prev is not None:
                                dd.append(ttp_prev)
                            ttp = tp(PS2[:, 1024 + j * 128:1152 + j * 128],
                                     ch[:, tb * 128:(tb + 1) * 128],
                                     deps=dd, sig=(j == 3))
                        tvv = vcopy(T1A, PS2[:, 1024:1536],
                                    deps=[ttp, tend], sig=True)
                        ttp_prev = tvv
                        dst = v_sc[kv, half * HALF + tb4 * 512:
                                   half * HALF + (tb4 + 1) * 512, :]
                        tend = dma("gpsimd",
                                   dst.rearrange("(j p) d -> p j d", p=128),
                                   T1A.rearrange("p (j d) -> p j d", j=4),
                                   deps=[tvv], ch="ST")
                return tend, ttp_prev

            tupv = proj(wupvt, CKV, NKV * HD, rhs_ckv, sink_upv, [tupk],
                        init_sink=tupk)

            def projq(wdeps, init_sink):
                wtok = [None, None]
                ev_prev = None
                sink_tok = [init_sink, init_sink]
                last_sink = init_sink
                prev_tro = None
                for ms in range(8):
                    stg = STAGE[ms % 2]
                    for ks in range(32):
                        slot = ks % 2
                        piece = wqt[ks * 128:(ks + 1) * 128,
                                    ms * 512:(ms + 1) * 512]
                        d = list(wdeps)
                        if wtok[slot] is not None:
                            d.append(wtok[slot])
                        tw = dma("sync", WBUF[slot], piece, deps=d,
                                 ch=("W0", "W1")[slot])
                        lastmm = None
                        for mi in range(4):
                            dd = [tw]
                            if ks == 0:
                                dd.append(ev_prev)
                            lastmm = mm(PS1[:, mi * 512:(mi + 1) * 512],
                                        WBUF[slot][:, mi * 128:(mi + 1) * 128],
                                        hcol(ks)[:, 0:512],
                                        start=(ks == 0), stop=(ks == 31),
                                        deps=dd, sig=(mi == 3))
                        wtok[slot] = lastmm
                    ev = None
                    for half2 in range(2):
                        ev = vcopy(
                            stg.rearrange("p (m t) -> p m t", t=512)
                            [:, half2 * 2:half2 * 2 + 2, :],
                            PS1[:, half2 * 1024:(half2 + 1) * 1024]
                            .rearrange("p (m t) -> p m t", t=512),
                            deps=[wtok[1], sink_tok[ms % 2]], sig=True)
                    ev_prev = ev
                    tend = ev
                    for mi in range(4):
                        g = ms * 4 + mi
                        ch = stg[:, mi * 512:(mi + 1) * 512]
                        dd = [ev] + trope
                        if prev_tro is not None:
                            dd.append(prev_tro)
                        tsw = mm(PS2[:, 1024:1536], PQM, ch, start=True,
                                 stop=True, deps=dd, sig=True)
                        t1 = vtt(T1A, ch, COSQ, ALU.mult,
                                 deps=[ev] + trope, sig=True)
                        t2 = vtt(TMP2A, PS2[:, 1024:1536], SINQ,
                                 ALU.mult, deps=[tsw], sig=True)
                        prev_tro = vtt(ch, T1A, TMP2A, ALU.add,
                                       deps=[t1, t2], sig=True)
                        tend = dma("gpsimd",
                                   q_sc[g * 128:(g + 1) * 128,
                                        half * 512:(half + 1) * 512],
                                   ch, deps=[prev_tro], ch="ST")
                    last_sink = tend
                    sink_tok[ms % 2] = last_sink
                return last_sink

            return projq([th, tupv], tupv)

        KBUF = PX.ap()[:, 0:2048]
        VBUF = PX.ap()[:, 2048:4096]
        QBUF = PX.ap()[:, 4096:6144]
        PSB = PW.ap()[:, 0:2048]
        PTSB = [PW.ap()[:, 2048:2176], PW.ap()[:, 2176:2304]]
        AOTM = PW.ap()[:, 2304:2432]
        AOST = PW.ap()[:, 2432:2944]

        def phase_B(p):
            joinB = s.join()
            TAILP = v_tails[:, p:p + 1].broadcast_to([128, 512])
            Lmax = 1024 if p == 0 else 2048
            lastk = joinB
            prev_texp = None
            prev_tmx = None
            prev_tao = None
            prev_tev2 = None
            for kv in range(NKV):
                dk = lastk if isinstance(lastk, list) else [lastk]
                tk = dma("sync", KBUF[:, 0:Lmax], k_sc[kv, :, 0:Lmax], deps=dk,
                         ch="W0")
                tv = dma("sync",
                         VBUF[:, 0:Lmax].rearrange("p (j d) -> p j d", d=128),
                         v_sc[kv, 0:Lmax, :].rearrange("(j p) d -> p j d",
                                                       p=128), deps=dk,
                         ch="W1")
                tq = dma("sync",
                         QBUF.rearrange("p (h t) -> p h t", h=4),
                         q_sc[kv * 512:(kv + 1) * 512,
                              p * 512:(p + 1) * 512].rearrange(
                                  "(h p) t -> p h t", p=128), deps=dk,
                         ch="X0")
                slot_last = []
                for sl in range(4):
                    sidx = p * 4 + sl
                    if p == 0:
                        part1 = list(range(sl + 1))
                        part2 = [4, 5, 6, 7]
                        diagpos = sl
                    else:
                        part1 = list(range(5 + sidx))
                        part2 = [12, 13, 14, 15]
                        diagpos = 4 + sidx
                    blocks = part1 + part2
                    nblk = len(blocks)
                    L = nblk * 128
                    head_last = []
                    for hh in range(4):
                        qt = QBUF[:, hh * 512 + sl * 128:
                                  hh * 512 + (sl + 1) * 128]
                        tsc = None
                        runs = []
                        off = 0
                        n1 = len(part1)
                        for c0 in range(0, n1, 4):
                            w = min(4, n1 - c0)
                            runs.append((off, part1[c0], w))
                            off += w
                        po2, b02, w2 = off, part2[0], 4
                        while w2 > 0:
                            w1 = min(w2, 4 - (po2 % 4))
                            runs.append((po2, b02, w1))
                            po2 += w1
                            b02 += w1
                            w2 -= w1
                        for (po, b0, w) in runs:
                            dd = [tk, tq]
                            if prev_texp is not None:
                                dd += [prev_texp, prev_tmx]
                            tsc = mm(PS1[:, po * 128:(po + w) * 128], qt,
                                     KBUF[:, b0 * 128:(b0 + w) * 128],
                                     start=True, stop=True, deps=dd,
                                     sig=(po + w == nblk))
                        tmk1 = vtt(PS1[:, diagpos * 128:(diagpos + 1) * 128],
                                   PS1[:, diagpos * 128:(diagpos + 1) * 128],
                                   DIAG, ALU.add, deps=[tsc], sig=True)
                        tmk2 = vtt(PS1[:, L - 512:L], PS1[:, L - 512:L],
                                   TAILP, ALU.add, deps=[tsc], sig=True)
                        prev_tmx = vred(v_m, PS1[:, 0:L], ALU.max,
                                        deps=[tsc, tmk1, tmk2], neg=True,
                                        sig=True)
                        tbb = vts(v_bi, v_m, INV_SQRT_HD, ALU.mult,
                                  deps=[prev_texp, prev_tmx], sig=True)
                        texp = act(PSB[:, 0:L], PS1[:, 0:L], ACTF.Exp,
                                   bias=v_bi, scale=INV_SQRT_HD, accum=v_sm,
                                   deps=[tbb, tsc, prev_tmx], sig=True)
                        prev_texp = texp
                        trc = vrecip(v_rc, v_sm, deps=[texp, prev_tao],
                                     sig=True)
                        tmmv = None
                        slot_busy = [None, None]
                        for j, blk in enumerate(blocks):
                            pslot = j % 2
                            dtp = [texp]
                            if slot_busy[pslot] is not None:
                                dtp.append(slot_busy[pslot])
                            ttp = tp(PS2[:, pslot * 128:(pslot + 1) * 128],
                                     PSB[:, j * 128:(j + 1) * 128],
                                     deps=dtp, sig=True)
                            tevp = vcopy(PTSB[pslot],
                                         PS2[:, pslot * 128:(pslot + 1) * 128],
                                         deps=[ttp], sig=True)
                            tmmv = mm(PS2[:, 512:640], PTSB[pslot],
                                      VBUF[:, blk * 128:(blk + 1) * 128],
                                      start=(j == 0), stop=(j == nblk - 1),
                                      deps=[tevp, tv], sig=(j == nblk - 1))
                            slot_busy[pslot] = tevp
                        tao = act(AOTM, PS2[:, 512:640], ACTF.Identity,
                                  scale=v_rc, deps=[tmmv, trc], sig=True)
                        prev_tao = tao
                        dd = [tao]
                        if prev_tev2 is not None:
                            dd.append(prev_tev2)
                        ttf = tp(PS2[:, 1024:1152], AOTM, deps=dd, sig=True)
                        dd = [ttf]
                        if len(slot_last) >= 1:
                            dd.append(slot_last[-1])
                        tev2 = vcopy(AOST[:, hh * 128:(hh + 1) * 128],
                                     PS2[:, 1024:1152], deps=dd, sig=True)
                        prev_tev2 = tev2
                        head_last.append(tev2)
                    dst = ao_sc[kv * 512:(kv + 1) * 512,
                                sidx * 128:(sidx + 1) * 128]
                    tst = dma("gpsimd",
                              dst.rearrange("(h p) c -> p h c", p=128),
                              AOST.rearrange("p (h c) -> p h c", h=4),
                              deps=head_last, ch="ST")
                    slot_last.append(tst)
                lastk = slot_last[-1]
            return lastk

        def phase_C(half):
            joinC = s.join()
            tao = dma("sync",
                      PH.ap().rearrange("p (c m) -> p c m", m=512)
                      [:, 0:32, :],
                      ao_sc.rearrange("(c p) m -> p c m", p=128)
                      [:, :, half * 512:(half + 1) * 512],
                      deps=joinC, ch="X0")
            YSTG = [PW.ap()[:, 1024:3072], PW.ap()[:, 3072:5120]]
            wtok = [None, None]
            ev_prev = [None, None]
            tr_prev = [None, None]
            last_tr = None
            for ms in range(8):
                psA = PS1 if ms % 2 == 0 else PS2
                psB = PS2 if ms % 2 == 0 else PS1
                lastmm = None
                for ks in range(32):
                    slot = ks % 2
                    piece = wot[ks * 128:(ks + 1) * 128,
                                ms * 512:(ms + 1) * 512]
                    d = [tao]
                    if wtok[slot] is not None:
                        d.append(wtok[slot])
                    tw = dma("sync", WBUF[slot], piece, deps=d,
                         ch=("W0", "W1")[slot])
                    for mi in range(4):
                        dd = [tw]
                        if ks == 0:
                            dd.append(ev_prev[ms % 2])
                            if ms >= 1:
                                dd.append(tr_prev[(ms - 1) % 2])
                        lastmm = mm(psA[:, mi * 512:(mi + 1) * 512],
                                    WBUF[slot][:, mi * 128:(mi + 1) * 128],
                                    PH.ap()[:, ks * 512:(ks + 1) * 512],
                                    start=(ks == 0), stop=(ks == 31),
                                    deps=dd, sig=(mi == 3))
                    wtok[slot] = lastmm
                tev = None
                for half2 in range(2):
                    tev = vcopy(
                        YSTG[ms % 2].rearrange("p (m t) -> p m t", t=512)
                        [:, half2 * 2:half2 * 2 + 2, :],
                        psA[:, half2 * 1024:(half2 + 1) * 1024]
                        .rearrange("p (m t) -> p m t", t=512),
                        deps=[lastmm, tr_prev[ms % 2]], sig=True)
                ev_prev[ms % 2] = tev
                for tb in range(4):
                    ttp = None
                    for mi in range(4):
                        dd = [tev]
                        if last_tr is not None:
                            dd.append(last_tr)
                        ttp = tp(psB[:, 1024 + mi * 128:1152 + mi * 128],
                                 YSTG[ms % 2][:, mi * 512 + tb * 128:
                                              mi * 512 + (tb + 1) * 128],
                                 deps=dd, sig=(mi == 3))
                    last_tr = vcopy(
                        PH3_4096[:, 4 + tb, ms * 512:(ms + 1) * 512],
                        psB[:, 1024:1536], deps=[ttp], sig=True)
                tr_prev[ms % 2] = last_tr
            tlast = None
            for tb in range(4):
                yt = PH3_4096[:, 4 + tb, :]
                txl = dma("sync", XTbuf[0],
                          x_own[half * 512 + tb * 128:
                                half * 512 + (tb + 1) * 128, :],
                          deps=[tlast] if tlast else [last_tr], ch="X1")
                tap, _, _ = layernorm(yt, yt, [last_tr, txl],
                                      sq_region=PS1[:, 0:1024])
                tadd = vtt(yt, yt, XTbuf[0], ALU.add, deps=[tap, txl],
                           sig=True)
                tlast = dma("gpsimd",
                            x1_sc[half * 512 + tb * 128:
                                  half * 512 + (tb + 1) * 128, :],
                            yt, deps=[tadd], ch="ST")
            return tlast

        def phase_D0():
            joinD = s.join()
            tm = vmemset(v_acc, 0.0, deps=joinD, sig=True)
            WCH = [PW.ap()[:, 1024:5120], PX.ap()[:, 0:4096]]
            tred = [tm, tm]
            tacc = tm
            for ks in range(32):
                slot = ks % 2
                tw = dma("sync", WCH[slot], wft[ks * 128:(ks + 1) * 128, :],
                         deps=(joinD + [tred[slot]] if ks >= 2
                               else joinD), ch=("W0", "W1")[slot])
                t1 = vred(va_d0, WCH[slot], ALU.add, deps=[tw, tacc],
                          absv=True, sig=True)
                tred[slot] = t1
                tacc = vtt(v_acc, v_acc, va_d0, ALU.add, deps=[t1, tacc],
                           sig=True)
            trep = mm(PS2[:, 0:1], ONES, v_acc, start=True, stop=True,
                      deps=[tacc], sig=True)
            tg0 = vts(v_gmc, PS2[:, 0:1], 1.0 / 16777216.0, ALU.mult,
                      deps=[trep], sig=True)
            tgm = vts(v_gmc, v_gmc, EPS, ALU.max, deps=[tg0], sig=True)
            tth = vts(v_th, v_gmc, 0.5, ALU.mult, deps=[tgm], sig=True)
            ttn = vts(v_thn, v_th, -1.0, ALU.mult, deps=[tth], sig=True)
            return [tgm, tth, ttn]

        def phase_D(half, tth):
            joinD = s.join()
            X1T = PX.ap()[:, 0:4096]
            FT = PX.ap()[:, 4096:8192]
            U0 = PH.ap()[:, 16384:20480]
            UU = PH.ap()[:, 20480:24576]
            GG = PH.ap()[:, 24576:28672]
            BMAT = PW.ap()[:, 1024:1536]
            last_t = None
            for tb in range(4):
                tx1 = dma("sync", X1T,
                          x1_sc[half * 512 + tb * 128:
                                half * 512 + (tb + 1) * 128, :],
                          deps=(joinD if tb == 0 else [last_t]), ch="X0")
                tf, _, _ = layernorm(X1T, FT, [tx1],
                                     rstd=rstd3[half * 4 + tb],
                                     biasv=bias3[half * 4 + tb])
                tsq = None
                for i in range(4):
                    tsq = act(sq_ps, FT[:, i * 1024:(i + 1) * 1024],
                              ACTF.Square, accum=v_ssa[i], deps=[tf],
                              sig=(i == 3))
                d1 = vtt(v_ssa[0], v_ssa[0], v_ssa[1], ALU.add, deps=[tsq],
                         sig=True)
                d2 = vtt(v_ssa[2], v_ssa[2], v_ssa[3], ALU.add, deps=[tsq],
                         sig=True)
                d3 = vtt(v_ssa[0], v_ssa[0], v_ssa[2], ALU.add,
                         deps=[d1, d2], sig=True)
                tss = vts(v_ssa[0], v_ssa[0], 1.0 / 4096.0, ALU.mult,
                          deps=[d3], sig=True)
                tr3 = act(v_r3, v_ssa[0], ACTF.Sqrt, bias=v_eps, deps=[tss],
                          sig=True)
                tr3 = vrecip(v_r3, v_r3, deps=[tr3], sig=True)
                tam = vred(v_am, FT, ALU.max, deps=[tf], absv=True, sig=True)
                d4 = vtt(v_am, v_am, v_r3, ALU.mult, deps=[tam, tr3],
                         sig=True)
                tc = vts(v_am, v_am, EPS, ALU.max, deps=[d4], sig=True)
                t5 = vts(v_rcd, v_am, 1.0 / 127.0, ALU.mult, deps=[tc],
                         sig=True)
                tsvg = vtt(svg3[half * 4 + tb], v_rcd, v_gmc, ALU.mult,
                           deps=[t5] + tth, sig=True)
                trcd = vrecip(v_rcd, v_am, deps=[tc, tsvg], sig=True)
                tsv = vstt(v_rcd, v_rcd, 127.0, v_r3, ALU.mult, ALU.mult,
                           deps=[trcd, tr3], sig=True)
                tu0 = act(U0, FT, ACTF.Identity, bias=v_1285, scale=v_rcd,
                          deps=[tsv, last_t], sig=True)
                tu = vcopy(UU, U0, deps=[tu0], sig=True)
                for bit in (128.0, 64.0, 32.0, 16.0, 8.0, 4.0, 2.0, 1.0):
                    tg = vts(GG, UU, bit, ALU.is_ge, deps=[tu], sig=True)
                    tu = vstt(UU, GG, -bit, UU, ALU.mult, ALU.add, deps=[tg],
                              sig=True)
                trr = vstt(GG, U0, -128.0, UU, ALU.add, ALU.subtract,
                           deps=[tu, tu0], sig=True)
                for g4 in range(8):
                    ttp = None
                    for j in range(4):
                        kc = g4 * 4 + j
                        ttp = tp(PS2[:, 1024 + j * 128:1152 + j * 128],
                                 GG[:, kc * 128:(kc + 1) * 128],
                                 deps=([trr, last_t] if last_t else [trr]),
                                 sig=(j == 3))
                    dst = PH.ap().rearrange("p (c m) -> p c m", m=512)[
                        :, g4 * 4:(g4 + 1) * 4, tb * 128:(tb + 1) * 128]
                    last_t = vcopy(dst, PS2[:, 1024:1536].rearrange(
                        "p (a m) -> p a m", a=4), deps=[ttp], sig=True)
            txq_done = last_t

            G1 = PW.ap()[:, 2048:2560]
            wtok = [None, None]
            ttr = [None, None]
            xconv = [None, None]
            last_ep = [None, None]
            geprev = None
            for ms in range(8):
                psA = PS1 if ms % 2 == 0 else PS2
                tbm = dma("sync", BMAT, bfull[:, ms * 512:(ms + 1) * 512],
                          deps=[geprev] if geprev else [txq_done], ch="BM")
                lastmm = [None] * 4
                for ks in range(32):
                    slot = ks % 2
                    piece = wft[ks * 128:(ks + 1) * 128,
                                ms * 512:(ms + 1) * 512]
                    d = [txq_done] + tth
                    if ttr[slot] is not None:
                        d.append(ttr[slot])
                    tw = dma("sync", WBUF[slot], piece, deps=d,
                         ch=("W0", "W1")[slot])
                    dtern = [tw] + tth
                    if wtok[slot] is not None:
                        dtern.append(wtok[slot])
                    tg1 = vts(G1, WBUF[slot], v_th, ALU.is_ge, deps=dtern,
                              sig=True)
                    tg2 = vts(TSTB[slot], WBUF[slot], v_thn, ALU.is_le,
                              deps=[tw, wtok[slot]], sig=True)
                    tg3 = vtt(TSTB[slot], G1, TSTB[slot], ALU.subtract,
                              deps=[tg1, tg2], sig=True)
                    ttr[slot] = tg3
                    dcv = [txq_done]
                    if xconv[slot] is not None:
                        dcv.append(xconv[slot])
                    tcv = vcopy(XQB[slot],
                                PH.ap()[:, ks * 512:(ks + 1) * 512],
                                deps=dcv, sig=True)
                    lm = None
                    for tb in range(4):
                        dd = [tg3, tcv]
                        if ks == 0:
                            dd.append(last_ep[ms % 2])
                        lm = mm(psA[:, tb * 512:(tb + 1) * 512],
                                XQB[slot][:, tb * 128:(tb + 1) * 128],
                                TSTB[slot], start=(ks == 0), stop=(ks == 31),
                                deps=dd, sig=(tb == 3))
                        lastmm[tb] = lm
                    wtok[slot] = lm
                    xconv[slot] = lm
                tep = None
                for tb in range(4):
                    tyl = vstt(T1A, psA[:, tb * 512:(tb + 1) * 512],
                               svg3[half * 4 + tb], BMAT, ALU.mult, ALU.add,
                               deps=[lastmm[3], tbm, geprev], sig=True)
                    geprev = act(PH3_4096[:, 4 + tb,
                                          ms * 512:(ms + 1) * 512],
                                 T1A, ACTF.Gelu, deps=[tyl], sig=True)
                    tep = geprev
                last_ep[ms % 2] = tep
            last_tr = tep

            tlast = None
            for tb in range(4):
                gt = PH3_4096[:, 4 + tb, :]
                tx1 = dma("sync", X1T,
                          x1_sc[half * 512 + tb * 128:
                                half * 512 + (tb + 1) * 128, :],
                          deps=([last_tr] if tb == 0 else [tlast]), ch="X0")
                tfa = act(FT, X1T, ACTF.Identity, bias=bias3[half * 4 + tb],
                          scale=rstd3[half * 4 + tb], deps=[tx1, last_tr],
                          sig=True)
                tsum = vtt(FT, FT, gt, ALU.add, deps=[tfa, last_tr], sig=True)
                tap, _, _ = layernorm(FT, FT, [tsum])
                tout = vtt(FT, FT, X1T, ALU.add, deps=[tap], sig=True)
                tlast = dma("gpsimd",
                            out[half * 512 + tb * 128:
                                half * 512 + (tb + 1) * 128, :],
                            FT, deps=[tout], ch="ST")
            return tlast

        phase_A(0)
        phase_B(0)
        phase_A(1)
        phase_B(1)
        phase_C(0)
        phase_C(1)
        tth = phase_D0()
        phase_D(0, tth)
        phase_D(1, tth)

        s.op("vector", (lambda e: e.engine_nop()),
             deps=[("S0", s.cnt["S0"]), ("S1", s.cnt["S1"])])

        with nc.Block() as blk:
            def make(engname):
                def run(eng):
                    for wl, emit, tok in s.stream[engname]:
                        for (sm, v) in wl:
                            eng.wait_ge(sem[sm], v)
                        inst = emit(eng)
                        if tok is not None:
                            inst.then_inc(sem[tok[0]],
                                          16 if tok[0] in DMA_CHANNELS else 1)
                return run

            blk.sync(make("sync"))
            blk.gpsimd(make("gpsimd"))
            blk.tensor(make("tensor"))
            blk.vector(make("vector"))
            blk.scalar(make("scalar"))

    return nc


def _host_tables():
    HEAD_DIM = 128
    ROPE_THETA = 10000.0
    inv_freq = 1.0 / ROPE_THETA ** (np.arange(0, HEAD_DIM, 2,
                                              dtype=np.float32) / HEAD_DIM)
    pos = np.arange(S, dtype=np.float32)
    freqs = np.outer(pos, inv_freq)
    emb = np.concatenate([freqs, freqs], axis=-1)
    cos = np.cos(emb).astype(np.float32)
    sin = np.sin(emb).astype(np.float32)
    return cos, sin


def kernel(L0_attn_hidden_states, L0_attn_Wq, L0_attn_Wkr, L0_attn_Wdk,
           L0_attn_Wupk, L0_attn_Wupv, L0_attn_Wo, L0_ffn_W, L0_ffn_b):
    X = np.ascontiguousarray(np.asarray(L0_attn_hidden_states, np.float32))
    WqT = np.ascontiguousarray(np.asarray(L0_attn_Wq, np.float32).T)
    WkrT = np.ascontiguousarray(np.asarray(L0_attn_Wkr, np.float32).T)
    WdkT = np.ascontiguousarray(np.asarray(L0_attn_Wdk, np.float32).T)
    WupkT = np.ascontiguousarray(np.asarray(L0_attn_Wupk, np.float32).T)
    WupvT = np.ascontiguousarray(np.asarray(L0_attn_Wupv, np.float32).T)
    WoT = np.ascontiguousarray(np.asarray(L0_attn_Wo, np.float32).T)
    WfT = np.ascontiguousarray(np.asarray(L0_ffn_W, np.float32).T)
    bfl = np.ascontiguousarray(np.broadcast_to(
        np.asarray(L0_ffn_b, np.float32)[None, :], (128, D)))

    cos, sin = _host_tables()
    ident = np.eye(128, dtype=np.float32)
    ones = np.ones((128, 128), dtype=np.float32)
    diag = np.where(np.arange(128)[None, :] <= np.arange(128)[:, None],
                    np.float32(0.0), np.float32(NEG1E9)).astype(np.float32)
    pqm = np.zeros((128, 128), np.float32)
    for i in range(64):
        pqm[i, 64 + i] = 1.0
        pqm[64 + i, i] = 1.0
    pkm = np.zeros((128, 128), np.float32)
    for base in (0, 64):
        for i in range(32):
            pkm[base + i, base + 32 + i] = 1.0
            pkm[base + 32 + i, base + i] = 1.0

    cosq_rows = np.ascontiguousarray(cos.T)
    sinq_rows = sin.T.copy()
    sinq_rows[0:64] *= -1.0
    kmap = np.zeros(128, dtype=np.int64)
    for r in range(128):
        rr = r % 64
        kmap[r] = rr if rr < 32 else 64 + (rr - 32)
    cosk_rows = np.ascontiguousarray(cos.T[kmap])
    sink_rows = sin.T[kmap].copy()
    for r in range(128):
        if (r % 64) < 32:
            sink_rows[r] *= -1.0

    nc = build_program()
    in_maps = []
    own_blocks_all = []
    for c in range(8):
        b = c // 2
        odd = c % 2
        h0 = [4, 5, 6, 7, 0, 1, 2, 3] if odd else [0, 1, 2, 3, 4, 5, 6, 7]
        h1 = ([8, 9, 10, 11, 12, 13, 14, 15] if odd
              else [12, 13, 14, 15, 8, 9, 10, 11])
        own_blocks_all.append(h0[:4] + h1[:4])
        perm0 = np.concatenate([np.arange(blk * 128, (blk + 1) * 128)
                                for blk in h0])
        perm1 = np.concatenate([np.arange(blk * 128, (blk + 1) * 128)
                                for blk in h1])
        xb = X[b]
        tail = np.zeros((128, 2), np.float32)
        tail[:, 0 if not odd else 1] = np.float32(NEG1E9)
        m = {
            "x_h0": np.ascontiguousarray(xb[perm0]),
            "x_h1": np.ascontiguousarray(xb[perm1]),
            "x_own": np.ascontiguousarray(
                np.concatenate([xb[perm0[:512]], xb[perm1[:512]]])),
            "wqt": WqT, "wkrt": WkrT, "wdkt": WdkT, "wupkt": WupkT,
            "wupvt": WupvT, "wot": WoT, "wft": WfT, "bfull": bfl,
            "identc": ident, "diagm": diag, "pq": pqm, "pk": pkm,
            "onesc": ones, "tailmask": tail,
            "cosq0": np.ascontiguousarray(cosq_rows[:, perm0[:512]]),
            "sinq0": np.ascontiguousarray(sinq_rows[:, perm0[:512]]),
            "cosq1": np.ascontiguousarray(cosq_rows[:, perm1[:512]]),
            "sinq1": np.ascontiguousarray(sinq_rows[:, perm1[:512]]),
            "cosk0": np.ascontiguousarray(cosk_rows[:, perm0]),
            "sink0": np.ascontiguousarray(sink_rows[:, perm0]),
            "cosk1": np.ascontiguousarray(cosk_rows[:, perm1]),
            "sink1": np.ascontiguousarray(sink_rows[:, perm1]),
        }
        in_maps.append(m)

    res = run_bass_kernel_spmd(nc, in_maps, core_ids=list(range(8)))
    full = np.empty((B, S, D), np.float32)
    for c in range(8):
        b = c // 2
        o = np.asarray(res.results[c]["out"]).reshape(8, 128, D)
        for i, blk in enumerate(own_blocks_all[c]):
            full[b, blk * 128:(blk + 1) * 128, :] = o[i]
    return full


# version 05
# speedup vs baseline: 1.1333x; 1.0932x over previous
import numpy as np
from contextlib import ExitStack

import concourse.bass as bass
import concourse.mybir as mybir
from concourse.bass_utils import run_bass_kernel_spmd

F32 = mybir.dt.float32
BF16 = mybir.dt.bfloat16
AXX = mybir.AxisListType.X
ALU = mybir.AluOpType
ACTF = mybir.ActivationFunctionType

B, S, D = 4, 2048, 4096
HALF = 1024
NKV = 8
HD = 128
CKV = 512
RD = 512
INV_SQRT_HD = 0.08838834764831845
NEG1E9 = -1.0e9
EPS = 1.0e-5

ENGINES = ("sync", "gpsimd", "tensor", "vector", "scalar")


DMA_CHANNELS = ("W0", "W1", "X0", "X1", "BM", "CT", "S0", "S1")


class Sched:

    def __init__(self):
        self.cnt = {"T": 0, "V": 0, "A": 0}
        for c in DMA_CHANNELS:
            self.cnt[c] = 0
        self.stream = {e: [] for e in ENGINES}
        self.lastw = {e: {} for e in ENGINES}
        self.own = {"tensor": "T", "vector": "V", "scalar": "A"}

    def op(self, eng, emit, deps=(), sig=False, ch=None):
        waits = {}
        for d in deps:
            if d is None:
                continue
            sm, v = d
            if v <= 0:
                continue
            if sm == "T" and self.own.get(eng) == "T":
                continue
            if v > waits.get(sm, 0):
                waits[sm] = v
        wl = []
        for sm, v in sorted(waits.items()):
            if self.lastw[eng].get(sm, 0) >= v:
                continue
            self.lastw[eng][sm] = v
            wl.append((sm, v))
        tok = None
        if sig:
            sm = ch if ch is not None else self.own[eng]
            amt = 16 if ch is not None else 1
            self.cnt[sm] += amt
            tok = (sm, self.cnt[sm])
        self.stream[eng].append([wl, emit, tok])
        return tok

    def retro_sig(self, eng):
        st = self.stream[eng]
        if not st:
            return None
        if st[-1][2] is None:
            sm = self.own[eng]
            self.cnt[sm] += 1
            st[-1][2] = (sm, self.cnt[sm])
        return st[-1][2]

    def join(self):
        toks = [t for t in (self.retro_sig(e)
                            for e in ("tensor", "vector", "scalar")) if t]
        for c in DMA_CHANNELS:
            if self.cnt[c] > 0:
                toks.append((c, self.cnt[c]))
        return toks


def build_program():
    nc = bass.Bass("TRN2", target_bir_lowering=False)

    def din(name, shape):
        return nc.dram_tensor(name, list(shape), F32, kind="ExternalInput").ap()

    xh = [din("x_h0", (HALF, D)), din("x_h1", (HALF, D))]
    x_own = din("x_own", (HALF, D))
    wqt = din("wqt", (D, D))
    wkrt = din("wkrt", (D, RD))
    wdkt = din("wdkt", (D, CKV))
    wupkt = din("wupkt", (CKV, CKV))
    wupvt = din("wupvt", (CKV, NKV * HD))
    wot = din("wot", (D, D))
    wft = din("wft", (D, D))
    bfull = din("bfull", (128, D))
    identc = din("identc", (128, 128))
    diagm = din("diagm", (128, 128))
    pqc = din("pq", (128, 128))
    pkc = din("pk", (128, 128))
    onesc = din("onesc", (128, 128))
    tailc = din("tailmask", (128, 2))
    cosq_in = [din("cosq0", (128, 512)), din("cosq1", (128, 512))]
    sinq_in = [din("sinq0", (128, 512)), din("sinq1", (128, 512))]
    cosk_in = [din("cosk0", (128, HALF)), din("cosk1", (128, HALF))]
    sink_in = [din("sink0", (128, HALF)), din("sink1", (128, HALF))]

    out = nc.dram_tensor("out", [HALF, D], F32, kind="ExternalOutput").ap()

    q_sc = nc.dram_tensor("q_sc", [D, HALF], F32, kind="Internal").ap()
    k_sc = nc.dram_tensor("k_sc", [NKV, 128, S], F32, kind="Internal").ap()
    v_sc = nc.dram_tensor("v_sc", [NKV, S, HD], F32, kind="Internal").ap()
    ao_sc = nc.dram_tensor("ao_sc", [D, HALF], F32, kind="Internal").ap()
    x1_sc = nc.dram_tensor("x1_sc", [HALF, D], F32, kind="Internal").ap()

    with ExitStack() as ctx:
        def sbuf(name, shape, dtype=F32):
            return ctx.enter_context(nc.sbuf_tensor(name, shape, dtype))

        PH = sbuf("PH", [128, 32768])
        PW = sbuf("PW", [128, 5632])
        PX = sbuf("PX", [128, 8192])
        PC = sbuf("PC", [128, 3712])
        PV = sbuf("PV", [128, 512])
        PB = sbuf("PB", [128, 4096], BF16)
        sem = {x: ctx.enter_context(nc.semaphore("sem_" + x))
               for x in (("T", "V", "A") + DMA_CHANNELS)}
        PS1 = ctx.enter_context(nc.psum_tensor("PS1", [128, 2048], F32))
        PS2 = ctx.enter_context(nc.psum_tensor("PS2", [128, 2048], F32))

        IDENT = PC.ap()[:, 0:128]
        DIAG = PC.ap()[:, 128:256]
        PQM = PC.ap()[:, 256:384]
        PKM = PC.ap()[:, 384:512]
        ONES = PC.ap()[:, 512:640]
        COSQ = PC.ap()[:, 640:1152]
        SINQ = PC.ap()[:, 1152:1664]
        COSK = PC.ap()[:, 1664:2688]
        SINKD = PC.ap()[:, 2688:3712]

        s = Sched()
        V1 = PV.ap()

        class Vec:
            def __init__(self):
                self.n = 0

            def slot(self, w=1):
                a = self.n
                self.n += w
                assert self.n <= 490, "vec pool overflow"
                return V1[:, a:a + w]

        vec = Vec()
        v_zero = vec.slot()
        v_tails = vec.slot(2)
        v_eps = vec.slot()
        v_1285 = vec.slot()
        v_acc = vec.slot()
        v_gmc = vec.slot()
        v_th = vec.slot()
        v_thn = vec.slot()
        rstd3 = [vec.slot() for _ in range(8)]
        bias3 = [vec.slot() for _ in range(8)]
        svg3 = [vec.slot() for _ in range(8)]
        v_m = V1[:, 494:495]
        v_sm = V1[:, 495:496]
        v_bi = V1[:, 496:497]
        v_rc = V1[:, 497:498]
        v_r3 = V1[:, 498:499]
        v_am = V1[:, 499:500]
        v_rcd = V1[:, 500:501]
        v_ssa = [V1[:, 501 + i:502 + i] for i in range(4)]
        va_d0 = V1[:, 505:506]

        st_rr = [0]

        def dma(queue, dst, src, deps=(), ch="CT"):
            if ch == "ST":
                ch = ("S0", "S1")[st_rr[0] % 2]
                st_rr[0] += 1
            deps = list(deps) + [(ch, s.cnt[ch])]
            return s.op(queue,
                        (lambda e, d=dst, so=src: e.dma_start(out=d, in_=so)),
                        deps, sig=True, ch=ch)

        def mm(outap, lhsT, rhs, start, stop, deps=(), sig=False):
            return s.op("tensor",
                        (lambda e, o=outap, l=lhsT, r=rhs, st=start, sp=stop:
                         e.matmul(o, l, r, start=st, stop=sp)), deps, sig=sig)

        def tp(outap, inap, deps=(), sig=False):
            return s.op("tensor",
                        (lambda e, o=outap, i=inap: e.transpose(o, i, IDENT)),
                        deps, sig=sig)

        def vcopy(dst, src, deps=(), sig=False):
            return s.op("vector",
                        (lambda e, d=dst, so=src: e.tensor_copy(d, so)),
                        deps, sig=sig)

        def vtt(dst, a, bb, op, deps=(), sig=False):
            return s.op("vector",
                        (lambda e, d=dst, x=a, y=bb, o=op:
                         e.tensor_tensor(d, x, y, op=o)), deps, sig=sig)

        def vts(dst, a, s1, op, deps=(), sig=False):
            return s.op("vector",
                        (lambda e, d=dst, x=a, c=s1, o=op:
                         e.tensor_single_scalar(d, x, c, op=o)), deps, sig=sig)

        def vstt(dst, in0, sc, in1, op0, op1, deps=(), sig=False):
            return s.op("vector",
                        (lambda e, d=dst, a=in0, c=sc, bb=in1, o0=op0, o1=op1:
                         e.scalar_tensor_tensor(d, a, c, bb, op0=o0, op1=o1)),
                        deps, sig=sig)

        def vred(dst, src, op, deps=(), sig=False, absv=False, neg=False):
            return s.op("vector",
                        (lambda e, d=dst, so=src, o=op, ab=absv, ng=neg:
                         e.tensor_reduce(d, so, axis=AXX, op=o,
                                         apply_absolute_value=ab, negate=ng)),
                        deps, sig=sig)

        def vmemset(dst, val, deps=(), sig=False):
            return s.op("vector", (lambda e, d=dst, v=val: e.memset(d, v)),
                        deps, sig=sig)

        def vrecip(dst, src, deps=(), sig=False):
            return s.op("vector",
                        (lambda e, d=dst, so=src: e.reciprocal(d, so)),
                        deps, sig=sig)

        def act(dst, src, func, bias=None, scale=1.0, accum=None, deps=(),
                sig=False):
            bias_ap = v_zero if bias is None else bias

            def emitf(e, d=dst, so=src, f=func, bb=bias_ap, sc=scale, au=accum):
                if au is None:
                    return e.activation(d, so, f, bias=bb, scale=sc)
                return e.activation(d, so, f, bias=bb, scale=sc, accum_out=au)
            return s.op("scalar", emitf, deps, sig=sig)

        dma("sync", IDENT, identc)
        dma("sync", DIAG, diagm)
        dma("sync", PQM, pqc)
        dma("sync", PKM, pkc)
        dma("sync", ONES, onesc)
        dma("sync", v_tails, tailc)
        vmemset(v_zero, 0.0)
        vmemset(v_eps, EPS)
        vmemset(v_1285, 128.5)
        tconst = s.join()

        sq_ps = PS2[:, 0:1024]

        def layernorm(xt, dst, ready, rstd=None, biasv=None,
                      sq_region=None):
            v_sum = vec.slot()
            vv = [vec.slot() for _ in range(4)]
            rst = rstd if rstd is not None else vec.slot()
            bia = biasv if biasv is not None else vec.slot()
            sq = sq_ps if sq_region is None else sq_region
            tsum = vred(v_sum, xt, ALU.add, deps=ready, sig=True)
            tsq = None
            for i in range(4):
                tsq = act(sq, xt[:, i * 1024:(i + 1) * 1024], ACTF.Square,
                          accum=vv[i], deps=ready, sig=(i == 3))
            c1 = vtt(vv[0], vv[0], vv[1], ALU.add, deps=[tsq], sig=True)
            c2 = vtt(vv[2], vv[2], vv[3], ALU.add, deps=[tsq], sig=True)
            c3 = vtt(vv[0], vv[0], vv[2], ALU.add, deps=[c1, c2], sig=True)
            c4 = vts(vv[0], vv[0], 1.0 / 4096.0, ALU.mult, deps=[c3],
                     sig=True)
            c5 = vts(v_sum, v_sum, 1.0 / 4096.0, ALU.mult, deps=[tsum],
                     sig=True)
            c6 = vtt(vv[1], v_sum, v_sum, ALU.mult, deps=[c5], sig=True)
            tvar = vtt(vv[0], vv[0], vv[1], ALU.subtract, deps=[c4, c6],
                       sig=True)
            tsd = act(rst, vv[0], ACTF.Sqrt, bias=v_eps, deps=[tvar],
                      sig=True)
            trst = vrecip(rst, rst, deps=[tsd], sig=True)
            tb = vstt(bia, v_sum, -1.0, rst, ALU.mult, ALU.mult,
                      deps=[trst, c5], sig=True)
            tap = act(dst, xt, ACTF.Identity, bias=bia, scale=rst, deps=[tb],
                      sig=True)
            return tap, rst, bia

        WBUF = [PW.ap()[:, 0:512], PW.ap()[:, 512:1024]]
        STAGE = [PW.ap()[:, 1024:5120], PX.ap()[:, 0:4096]]
        T1A = PW.ap()[:, 5120:5632]
        XTbuf = [PX.ap()[:, 0:4096], PX.ap()[:, 4096:8192]]
        CKVB = PX.ap()[:, 4096:8192]
        PH3_1024 = PH.ap().rearrange("p (c m) -> p c m", m=1024)
        PH3_4096 = PH.ap().rearrange("p (c m) -> p c m", m=4096)
        TSTB = [PB.ap()[:, 0:512], PB.ap()[:, 512:1024]]
        XQB = [PB.ap()[:, 1024:1536], PB.ap()[:, 1536:2048]]
        WHs = [PB.ap()[:, 0:512], PB.ap()[:, 512:1024]]
        WLs = [PB.ap()[:, 1024:1536], PB.ap()[:, 1536:2048]]
        XHs = [PB.ap()[:, 2048:2560], PB.ap()[:, 2560:3072]]
        XLs = [PB.ap()[:, 3072:3584], PB.ap()[:, 3584:4096]]

        def hcol(kc):
            return PH.ap()[:, kc * 1024:(kc + 1) * 1024]

        def phase_A(half):
            joinA = s.join()
            trope = [dma("sync", COSQ, cosq_in[half], deps=joinA),
                     dma("sync", SINQ, sinq_in[half]),
                     dma("sync", COSK, cosk_in[half]),
                     dma("sync", SINKD, sink_in[half])] + tconst

            h_done = []
            tile_tp = [None, None]
            for t in range(8):
                xt = XTbuf[t % 2]
                d = list(joinA)
                if tile_tp[t % 2] is not None:
                    d.append(tile_tp[t % 2])
                txl = dma("sync", xt, xh[half][t * 128:(t + 1) * 128, :],
                          deps=d, ch=("X0", "X1")[t % 2])
                tap, _, _ = layernorm(xt, xt, [txl])
                for g4 in range(8):
                    ttp = None
                    for j in range(4):
                        kc = g4 * 4 + j
                        ttp = tp(PS2[:, 1024 + j * 128:1152 + j * 128],
                                 xt[:, kc * 128:(kc + 1) * 128],
                                 deps=([tap, h_done[-1]] if h_done else [tap]),
                                 sig=(j == 3))
                    dst = PH3_1024[:, g4 * 4:(g4 + 1) * 4,
                                   t * 128:(t + 1) * 128]
                    tev = vcopy(dst, PS2[:, 1024:1536].rearrange(
                        "p (a m) -> p a m", a=4), deps=[ttp], sig=True)
                    h_done.append(tev)
                tile_tp[t % 2] = ttp
            th = h_done[-1]

            def proj(wt_ap, Kdim, Mout, rhs_fn, sink_fn, wdeps,
                     init_sink=None):
                nks = Kdim // 128
                wtok = [None, None]
                wsub = [None, None]
                xtok = [None, None]
                ev1_prev, ev2_prev = None, None
                ps2rel_prev = None
                sink_tok = [init_sink, init_sink]
                last_sink = init_sink
                for ms in range(Mout // 512):
                    stg = STAGE[ms % 2]
                    for ks in range(nks):
                        slot = ks % 2
                        piece = wt_ap[ks * 128:(ks + 1) * 128,
                                      ms * 512:(ms + 1) * 512]
                        d = list(wdeps)
                        if wsub[slot] is not None:
                            d.append(wsub[slot])
                        tw = dma("sync", WBUF[slot], piece, deps=d,
                                 ch=("W0", "W1")[slot])
                        twh = vcopy(WHs[slot], WBUF[slot],
                                    deps=[tw, wtok[slot]], sig=True)
                        twl = vtt(WLs[slot], WBUF[slot], WHs[slot],
                                  ALU.subtract, deps=[twh], sig=True)
                        wsub[slot] = twl
                        cx = [None, None]
                        for th_i in range(2):
                            xi = th_i
                            r = rhs_fn(ks, th_i)
                            c1 = vcopy(XHs[xi], r,
                                       deps=list(wdeps) + [xtok[xi]],
                                       sig=True)
                            cx[th_i] = vtt(XLs[xi], r, XHs[xi],
                                           ALU.subtract, deps=[c1], sig=True)
                        lastmm = None
                        for th_i in range(2):
                            xi = th_i
                            psb = PS1 if th_i == 0 else PS2
                            for mi in range(4):
                                for pi, (lw, rx) in enumerate(
                                        ((WHs[slot], XHs[xi]),
                                         (WHs[slot], XLs[xi]),
                                         (WLs[slot], XHs[xi]))):
                                    dd = [twl, cx[th_i]]
                                    if ks == 0 and pi == 0:
                                        dd.append(ev1_prev if th_i == 0
                                                  else ev2_prev)
                                        if th_i == 1:
                                            dd.append(ps2rel_prev)
                                    lastmm = mm(
                                        psb[:, mi * 512:(mi + 1) * 512],
                                        lw[:, mi * 128:(mi + 1) * 128], rx,
                                        start=(ks == 0 and pi == 0),
                                        stop=(ks == nks - 1 and pi == 2),
                                        deps=dd,
                                        sig=(mi == 3 and pi == 2))
                            xtok[xi] = lastmm
                        wtok[slot] = lastmm
                    mmtok = wtok[(nks - 1) % 2]
                    sdep = sink_tok[ms % 2]
                    ev1 = None
                    for half2 in range(2):
                        ev1 = vcopy(
                            stg.rearrange("p (m t) -> p m t", t=1024)
                            [:, half2 * 2:half2 * 2 + 2, 0:512],
                            PS1[:, half2 * 1024:(half2 + 1) * 1024]
                            .rearrange("p (m t) -> p m t", t=512),
                            deps=[mmtok, sdep], sig=True)
                    ev2 = None
                    for half2 in range(2):
                        ev2 = vcopy(
                            stg.rearrange("p (m t) -> p m t", t=1024)
                            [:, half2 * 2:half2 * 2 + 2, 512:1024],
                            PS2[:, half2 * 1024:(half2 + 1) * 1024]
                            .rearrange("p (m t) -> p m t", t=512),
                            deps=[mmtok, sdep], sig=True)
                    ev1_prev, ev2_prev = ev1, ev2
                    last_sink, ps2rel_prev = sink_fn(ms, ev2, stg,
                                                     ps2rel_prev)
                    sink_tok[ms % 2] = last_sink
                return last_sink

            def rhs_h(ks, th_i):
                return hcol(ks)[:, th_i * 512:(th_i + 1) * 512]

            def rhs_ckv(ks, th_i):
                return CKVB[:, ks * 1024 + th_i * 512:
                            ks * 1024 + th_i * 512 + 512]

            def sink_kr(ms, tev, stg, ps2rel):
                tend = tev
                prev_tro = [ps2rel, ps2rel]
                for mi in range(4):
                    ch = stg[:, mi * 1024:(mi + 1) * 1024]
                    tros = []
                    for th_i in range(2):
                        chh = ch[:, th_i * 512:(th_i + 1) * 512]
                        dd = [tev] + trope
                        if prev_tro[th_i] is not None:
                            dd.append(prev_tro[th_i])
                        tsw = mm(PS2[:, 1024 + th_i * 512:1536 + th_i * 512],
                                 PKM, chh, start=True, stop=True, deps=dd,
                                 sig=True)
                        t1 = vtt(T1A, chh,
                                 COSK[:, th_i * 512:(th_i + 1) * 512],
                                 ALU.mult, deps=[tev] + trope, sig=True)
                        t2 = vtt(PS2[:, 1024 + th_i * 512:1536 + th_i * 512],
                                 PS2[:, 1024 + th_i * 512:1536 + th_i * 512],
                                 SINKD[:, th_i * 512:(th_i + 1) * 512],
                                 ALU.mult, deps=[tsw], sig=True)
                        tro = vtt(chh, T1A,
                                  PS2[:, 1024 + th_i * 512:1536 + th_i * 512],
                                  ALU.add, deps=[t1, t2], sig=True)
                        prev_tro[th_i] = tro
                        tros.append(tro)
                    ka, kb = 2 * mi, 2 * mi + 1
                    for (r0, kv, dr) in ((0, ka, 0), (32, ka, 64),
                                         (64, kb, 0), (96, kb, 64)):
                        tend = dma("gpsimd",
                                   k_sc[kv, dr:dr + 32,
                                        half * HALF:(half + 1) * HALF],
                                   ch[r0:r0 + 32, :], deps=tros, ch="ST")
                return tend, prev_tro[1]

            tkr = proj(wkrt, D, RD, rhs_h, sink_kr, [th])

            def sink_dk(ms, tev, stg, ps2rel):
                t = None
                for mi in range(4):
                    t = vcopy(CKVB[:, mi * 1024:(mi + 1) * 1024],
                              stg[:, mi * 1024:(mi + 1) * 1024],
                              deps=[tev], sig=(mi == 3))
                return t, ps2rel

            tdk = proj(wdkt, D, CKV, rhs_h, sink_dk, [th, tkr],
                       init_sink=tkr)

            def sink_upk(ms, tev, stg, ps2rel):
                tend = tev
                for mi in range(4):
                    ch = stg[:, mi * 1024:(mi + 1) * 1024]
                    ka, kb = 2 * mi, 2 * mi + 1
                    for (r0, kv, dr) in ((0, ka, 32), (32, ka, 96),
                                         (64, kb, 32), (96, kb, 96)):
                        tend = dma("gpsimd",
                                   k_sc[kv, dr:dr + 32,
                                        half * HALF:(half + 1) * HALF],
                                   ch[r0:r0 + 32, :], deps=[tev], ch="ST")
                return tend, ps2rel

            tupk = proj(wupkt, CKV, CKV, rhs_ckv, sink_upk, [tdk],
                        init_sink=tdk)

            def sink_upv(ms, tev, stg, ps2rel):
                tend = tev
                ttp_prev = ps2rel
                for mi in range(4):
                    kv = ms * 4 + mi
                    ch = stg[:, mi * 1024:(mi + 1) * 1024]
                    for tb4 in range(2):
                        ttp = None
                        for j in range(4):
                            tb = tb4 * 4 + j
                            dd = [tev]
                            if ttp_prev is not None:
                                dd.append(ttp_prev)
                            ttp = tp(PS2[:, 1024 + j * 128:1152 + j * 128],
                                     ch[:, tb * 128:(tb + 1) * 128],
                                     deps=dd, sig=(j == 3))
                        tvv = vcopy(T1A, PS2[:, 1024:1536],
                                    deps=[ttp, tend], sig=True)
                        ttp_prev = tvv
                        dst = v_sc[kv, half * HALF + tb4 * 512:
                                   half * HALF + (tb4 + 1) * 512, :]
                        tend = dma("gpsimd",
                                   dst.rearrange("(j p) d -> p j d", p=128),
                                   T1A.rearrange("p (j d) -> p j d", j=4),
                                   deps=[tvv], ch="ST")
                return tend, ttp_prev

            tupv = proj(wupvt, CKV, NKV * HD, rhs_ckv, sink_upv, [tupk],
                        init_sink=tupk)

            def projq(wdeps, init_sink):
                wtok = [None, None]
                wsub = [None, None]
                xtok = [None, None]
                ev_prev = None
                sink_tok = [init_sink, init_sink]
                last_sink = init_sink
                prev_tro = None
                for ms in range(8):
                    stg = STAGE[ms % 2]
                    for ks in range(32):
                        slot = ks % 2
                        piece = wqt[ks * 128:(ks + 1) * 128,
                                    ms * 512:(ms + 1) * 512]
                        d = list(wdeps)
                        if wsub[slot] is not None:
                            d.append(wsub[slot])
                        tw = dma("sync", WBUF[slot], piece, deps=d,
                                 ch=("W0", "W1")[slot])
                        twh = vcopy(WHs[slot], WBUF[slot],
                                    deps=[tw, wtok[slot]], sig=True)
                        twl = vtt(WLs[slot], WBUF[slot], WHs[slot],
                                  ALU.subtract, deps=[twh], sig=True)
                        wsub[slot] = twl
                        r = hcol(ks)[:, 0:512]
                        c1 = vcopy(XHs[slot], r,
                                   deps=list(wdeps) + [xtok[slot]], sig=True)
                        cx = vtt(XLs[slot], r, XHs[slot], ALU.subtract,
                                 deps=[c1], sig=True)
                        lastmm = None
                        for mi in range(4):
                            for pi, (lw, rx) in enumerate(
                                    ((WHs[slot], XHs[slot]),
                                     (WHs[slot], XLs[slot]),
                                     (WLs[slot], XHs[slot]))):
                                dd = [twl, cx]
                                if ks == 0 and pi == 0:
                                    dd.append(ev_prev)
                                lastmm = mm(PS1[:, mi * 512:(mi + 1) * 512],
                                            lw[:, mi * 128:(mi + 1) * 128],
                                            rx,
                                            start=(ks == 0 and pi == 0),
                                            stop=(ks == 31 and pi == 2),
                                            deps=dd,
                                            sig=(mi == 3 and pi == 2))
                        wtok[slot] = lastmm
                        xtok[slot] = lastmm
                    ev = None
                    for half2 in range(2):
                        ev = vcopy(
                            stg.rearrange("p (m t) -> p m t", t=512)
                            [:, half2 * 2:half2 * 2 + 2, :],
                            PS1[:, half2 * 1024:(half2 + 1) * 1024]
                            .rearrange("p (m t) -> p m t", t=512),
                            deps=[wtok[1], sink_tok[ms % 2]], sig=True)
                    ev_prev = ev
                    tend = ev
                    for mi in range(4):
                        g = ms * 4 + mi
                        ch = stg[:, mi * 512:(mi + 1) * 512]
                        dd = [ev] + trope
                        if prev_tro is not None:
                            dd.append(prev_tro)
                        tsw = mm(PS2[:, 1024:1536], PQM, ch, start=True,
                                 stop=True, deps=dd, sig=True)
                        t1 = vtt(T1A, ch, COSQ, ALU.mult,
                                 deps=[ev] + trope, sig=True)
                        t2 = vtt(PS2[:, 1024:1536], PS2[:, 1024:1536], SINQ,
                                 ALU.mult, deps=[tsw], sig=True)
                        prev_tro = vtt(ch, T1A, PS2[:, 1024:1536], ALU.add,
                                       deps=[t1, t2], sig=True)
                        tend = dma("gpsimd",
                                   q_sc[g * 128:(g + 1) * 128,
                                        half * 512:(half + 1) * 512],
                                   ch, deps=[prev_tro], ch="ST")
                    last_sink = tend
                    sink_tok[ms % 2] = last_sink
                return last_sink

            return projq([th, tupv], tupv)

        KBUF = PX.ap()[:, 0:2048]
        VBUF = PX.ap()[:, 2048:4096]
        QBUF = PX.ap()[:, 4096:6144]
        PSB = PW.ap()[:, 0:2048]
        PTSB = [PW.ap()[:, 2048:2176], PW.ap()[:, 2176:2304]]
        AOTM = PW.ap()[:, 2304:2432]
        AOST = PW.ap()[:, 2432:2944]

        def phase_B(p):
            joinB = s.join()
            TAILP = v_tails[:, p:p + 1].broadcast_to([128, 512])
            Lmax = 1024 if p == 0 else 2048
            lastk = joinB
            prev_texp = None
            prev_tmx = None
            prev_tao = None
            prev_tev2 = None
            for kv in range(NKV):
                dk = lastk if isinstance(lastk, list) else [lastk]
                tk = dma("sync", KBUF[:, 0:Lmax], k_sc[kv, :, 0:Lmax], deps=dk,
                         ch="W0")
                tv = dma("sync",
                         VBUF[:, 0:Lmax].rearrange("p (j d) -> p j d", d=128),
                         v_sc[kv, 0:Lmax, :].rearrange("(j p) d -> p j d",
                                                       p=128), deps=dk,
                         ch="W1")
                tq = dma("sync",
                         QBUF.rearrange("p (h t) -> p h t", h=4),
                         q_sc[kv * 512:(kv + 1) * 512,
                              p * 512:(p + 1) * 512].rearrange(
                                  "(h p) t -> p h t", p=128), deps=dk,
                         ch="X0")
                slot_last = []
                for sl in range(4):
                    sidx = p * 4 + sl
                    if p == 0:
                        part1 = list(range(sl + 1))
                        part2 = [4, 5, 6, 7]
                        diagpos = sl
                    else:
                        part1 = list(range(5 + sidx))
                        part2 = [12, 13, 14, 15]
                        diagpos = 4 + sidx
                    blocks = part1 + part2
                    nblk = len(blocks)
                    L = nblk * 128
                    head_last = []
                    for hh in range(4):
                        qt = QBUF[:, hh * 512 + sl * 128:
                                  hh * 512 + (sl + 1) * 128]
                        tsc = None
                        runs = []
                        off = 0
                        n1 = len(part1)
                        for c0 in range(0, n1, 4):
                            w = min(4, n1 - c0)
                            runs.append((off, part1[c0], w))
                            off += w
                        po2, b02, w2 = off, part2[0], 4
                        while w2 > 0:
                            w1 = min(w2, 4 - (po2 % 4))
                            runs.append((po2, b02, w1))
                            po2 += w1
                            b02 += w1
                            w2 -= w1
                        for (po, b0, w) in runs:
                            dd = [tk, tq]
                            if prev_texp is not None:
                                dd += [prev_texp, prev_tmx]
                            tsc = mm(PS1[:, po * 128:(po + w) * 128], qt,
                                     KBUF[:, b0 * 128:(b0 + w) * 128],
                                     start=True, stop=True, deps=dd,
                                     sig=(po + w == nblk))
                        tmk1 = vtt(PS1[:, diagpos * 128:(diagpos + 1) * 128],
                                   PS1[:, diagpos * 128:(diagpos + 1) * 128],
                                   DIAG, ALU.add, deps=[tsc], sig=True)
                        tmk2 = vtt(PS1[:, L - 512:L], PS1[:, L - 512:L],
                                   TAILP, ALU.add, deps=[tsc], sig=True)
                        prev_tmx = vred(v_m, PS1[:, 0:L], ALU.max,
                                        deps=[tsc, tmk1, tmk2], neg=True,
                                        sig=True)
                        tbb = vts(v_bi, v_m, INV_SQRT_HD, ALU.mult,
                                  deps=[prev_texp, prev_tmx], sig=True)
                        texp = act(PSB[:, 0:L], PS1[:, 0:L], ACTF.Exp,
                                   bias=v_bi, scale=INV_SQRT_HD, accum=v_sm,
                                   deps=[tbb, tsc, prev_tmx], sig=True)
                        prev_texp = texp
                        trc = vrecip(v_rc, v_sm, deps=[texp, prev_tao],
                                     sig=True)
                        tmmv = None
                        slot_busy = [None, None]
                        for j, blk in enumerate(blocks):
                            pslot = j % 2
                            dtp = [texp]
                            if slot_busy[pslot] is not None:
                                dtp.append(slot_busy[pslot])
                            ttp = tp(PS2[:, pslot * 128:(pslot + 1) * 128],
                                     PSB[:, j * 128:(j + 1) * 128],
                                     deps=dtp, sig=True)
                            tevp = vcopy(PTSB[pslot],
                                         PS2[:, pslot * 128:(pslot + 1) * 128],
                                         deps=[ttp], sig=True)
                            tmmv = mm(PS2[:, 512:640], PTSB[pslot],
                                      VBUF[:, blk * 128:(blk + 1) * 128],
                                      start=(j == 0), stop=(j == nblk - 1),
                                      deps=[tevp, tv], sig=(j == nblk - 1))
                            slot_busy[pslot] = tevp
                        tao = act(AOTM, PS2[:, 512:640], ACTF.Identity,
                                  scale=v_rc, deps=[tmmv, trc], sig=True)
                        prev_tao = tao
                        dd = [tao]
                        if prev_tev2 is not None:
                            dd.append(prev_tev2)
                        ttf = tp(PS2[:, 1024:1152], AOTM, deps=dd, sig=True)
                        dd = [ttf]
                        if len(slot_last) >= 1:
                            dd.append(slot_last[-1])
                        tev2 = vcopy(AOST[:, hh * 128:(hh + 1) * 128],
                                     PS2[:, 1024:1152], deps=dd, sig=True)
                        prev_tev2 = tev2
                        head_last.append(tev2)
                    dst = ao_sc[kv * 512:(kv + 1) * 512,
                                sidx * 128:(sidx + 1) * 128]
                    tst = dma("gpsimd",
                              dst.rearrange("(h p) c -> p h c", p=128),
                              AOST.rearrange("p (h c) -> p h c", h=4),
                              deps=head_last, ch="ST")
                    slot_last.append(tst)
                lastk = slot_last[-1]
            return lastk

        def phase_C(half):
            joinC = s.join()
            tao = dma("sync",
                      PH.ap().rearrange("p (c m) -> p c m", m=512)
                      [:, 0:32, :],
                      ao_sc.rearrange("(c p) m -> p c m", p=128)
                      [:, :, half * 512:(half + 1) * 512],
                      deps=joinC, ch="X0")
            YSTG = [PW.ap()[:, 1024:3072], PW.ap()[:, 3072:5120]]
            wtok = [None, None]
            ev_prev = [None, None]
            tr_prev = [None, None]
            wsub = [None, None]
            xtok = [None, None]
            last_tr = None
            for ms in range(8):
                psA = PS1 if ms % 2 == 0 else PS2
                psB = PS2 if ms % 2 == 0 else PS1
                lastmm = None
                for ks in range(32):
                    slot = ks % 2
                    piece = wot[ks * 128:(ks + 1) * 128,
                                ms * 512:(ms + 1) * 512]
                    d = [tao]
                    if wsub[slot] is not None:
                        d.append(wsub[slot])
                    tw = dma("sync", WBUF[slot], piece, deps=d,
                         ch=("W0", "W1")[slot])
                    twh = vcopy(WHs[slot], WBUF[slot],
                                deps=[tw, wtok[slot]], sig=True)
                    twl = vtt(WLs[slot], WBUF[slot], WHs[slot],
                              ALU.subtract, deps=[twh], sig=True)
                    wsub[slot] = twl
                    r = PH.ap()[:, ks * 512:(ks + 1) * 512]
                    c1 = vcopy(XHs[slot], r, deps=[tao, xtok[slot]],
                               sig=True)
                    cx = vtt(XLs[slot], r, XHs[slot], ALU.subtract,
                             deps=[c1], sig=True)
                    for mi in range(4):
                        for pi, (lw, rx) in enumerate(
                                ((WHs[slot], XHs[slot]),
                                 (WHs[slot], XLs[slot]),
                                 (WLs[slot], XHs[slot]))):
                            dd = [twl, cx]
                            if ks == 0 and pi == 0:
                                dd.append(ev_prev[ms % 2])
                                if ms >= 1:
                                    dd.append(tr_prev[(ms - 1) % 2])
                            lastmm = mm(psA[:, mi * 512:(mi + 1) * 512],
                                        lw[:, mi * 128:(mi + 1) * 128], rx,
                                        start=(ks == 0 and pi == 0),
                                        stop=(ks == 31 and pi == 2),
                                        deps=dd, sig=(mi == 3 and pi == 2))
                    wtok[slot] = lastmm
                    xtok[slot] = lastmm
                tev = None
                for half2 in range(2):
                    tev = vcopy(
                        YSTG[ms % 2].rearrange("p (m t) -> p m t", t=512)
                        [:, half2 * 2:half2 * 2 + 2, :],
                        psA[:, half2 * 1024:(half2 + 1) * 1024]
                        .rearrange("p (m t) -> p m t", t=512),
                        deps=[lastmm, tr_prev[ms % 2]], sig=True)
                ev_prev[ms % 2] = tev
                for tb in range(4):
                    ttp = None
                    for mi in range(4):
                        dd = [tev]
                        if last_tr is not None:
                            dd.append(last_tr)
                        ttp = tp(psB[:, 1024 + mi * 128:1152 + mi * 128],
                                 YSTG[ms % 2][:, mi * 512 + tb * 128:
                                              mi * 512 + (tb + 1) * 128],
                                 deps=dd, sig=(mi == 3))
                    last_tr = vcopy(
                        PH3_4096[:, 4 + tb, ms * 512:(ms + 1) * 512],
                        psB[:, 1024:1536], deps=[ttp], sig=True)
                tr_prev[ms % 2] = last_tr
            tlast = None
            for tb in range(4):
                yt = PH3_4096[:, 4 + tb, :]
                txl = dma("sync", XTbuf[0],
                          x_own[half * 512 + tb * 128:
                                half * 512 + (tb + 1) * 128, :],
                          deps=[tlast] if tlast else [last_tr], ch="X1")
                tap, _, _ = layernorm(yt, yt, [last_tr, txl],
                                      sq_region=PS1[:, 0:1024])
                tadd = vtt(yt, yt, XTbuf[0], ALU.add, deps=[tap, txl],
                           sig=True)
                tlast = dma("gpsimd",
                            x1_sc[half * 512 + tb * 128:
                                  half * 512 + (tb + 1) * 128, :],
                            yt, deps=[tadd], ch="ST")
            return tlast

        def phase_D0():
            joinD = s.join()
            tm = vmemset(v_acc, 0.0, deps=joinD, sig=True)
            WCH = [PW.ap()[:, 1024:5120], PX.ap()[:, 0:4096]]
            tred = [tm, tm]
            tacc = tm
            for ks in range(32):
                slot = ks % 2
                tw = dma("sync", WCH[slot], wft[ks * 128:(ks + 1) * 128, :],
                         deps=(joinD + [tred[slot]] if ks >= 2
                               else joinD), ch=("W0", "W1")[slot])
                t1 = vred(va_d0, WCH[slot], ALU.add, deps=[tw, tacc],
                          absv=True, sig=True)
                tred[slot] = t1
                tacc = vtt(v_acc, v_acc, va_d0, ALU.add, deps=[t1, tacc],
                           sig=True)
            trep = mm(PS2[:, 0:1], ONES, v_acc, start=True, stop=True,
                      deps=[tacc], sig=True)
            tg0 = vts(v_gmc, PS2[:, 0:1], 1.0 / 16777216.0, ALU.mult,
                      deps=[trep], sig=True)
            tgm = vts(v_gmc, v_gmc, EPS, ALU.max, deps=[tg0], sig=True)
            tth = vts(v_th, v_gmc, 0.5, ALU.mult, deps=[tgm], sig=True)
            ttn = vts(v_thn, v_th, -1.0, ALU.mult, deps=[tth], sig=True)
            return [tgm, tth, ttn]

        def phase_D(half, tth):
            joinD = s.join()
            X1T = PX.ap()[:, 0:4096]
            FT = PX.ap()[:, 4096:8192]
            U0 = PH.ap()[:, 16384:20480]
            UU = PH.ap()[:, 20480:24576]
            GG = PH.ap()[:, 24576:28672]
            BMAT = PW.ap()[:, 1024:1536]
            last_t = None
            for tb in range(4):
                tx1 = dma("sync", X1T,
                          x1_sc[half * 512 + tb * 128:
                                half * 512 + (tb + 1) * 128, :],
                          deps=(joinD if tb == 0 else [last_t]), ch="X0")
                tf, _, _ = layernorm(X1T, FT, [tx1],
                                     rstd=rstd3[half * 4 + tb],
                                     biasv=bias3[half * 4 + tb])
                tsq = None
                for i in range(4):
                    tsq = act(sq_ps, FT[:, i * 1024:(i + 1) * 1024],
                              ACTF.Square, accum=v_ssa[i], deps=[tf],
                              sig=(i == 3))
                d1 = vtt(v_ssa[0], v_ssa[0], v_ssa[1], ALU.add, deps=[tsq],
                         sig=True)
                d2 = vtt(v_ssa[2], v_ssa[2], v_ssa[3], ALU.add, deps=[tsq],
                         sig=True)
                d3 = vtt(v_ssa[0], v_ssa[0], v_ssa[2], ALU.add,
                         deps=[d1, d2], sig=True)
                tss = vts(v_ssa[0], v_ssa[0], 1.0 / 4096.0, ALU.mult,
                          deps=[d3], sig=True)
                tr3 = act(v_r3, v_ssa[0], ACTF.Sqrt, bias=v_eps, deps=[tss],
                          sig=True)
                tr3 = vrecip(v_r3, v_r3, deps=[tr3], sig=True)
                tam = vred(v_am, FT, ALU.max, deps=[tf], absv=True, sig=True)
                d4 = vtt(v_am, v_am, v_r3, ALU.mult, deps=[tam, tr3],
                         sig=True)
                tc = vts(v_am, v_am, EPS, ALU.max, deps=[d4], sig=True)
                t5 = vts(v_rcd, v_am, 1.0 / 127.0, ALU.mult, deps=[tc],
                         sig=True)
                tsvg = vtt(svg3[half * 4 + tb], v_rcd, v_gmc, ALU.mult,
                           deps=[t5] + tth, sig=True)
                trcd = vrecip(v_rcd, v_am, deps=[tc, tsvg], sig=True)
                tsv = vstt(v_rcd, v_rcd, 127.0, v_r3, ALU.mult, ALU.mult,
                           deps=[trcd, tr3], sig=True)
                tu0 = act(U0, FT, ACTF.Identity, bias=v_1285, scale=v_rcd,
                          deps=[tsv, last_t], sig=True)
                tu = vcopy(UU, U0, deps=[tu0], sig=True)
                for bit in (128.0, 64.0, 32.0, 16.0, 8.0, 4.0, 2.0, 1.0):
                    tg = vts(GG, UU, bit, ALU.is_ge, deps=[tu], sig=True)
                    tu = vstt(UU, GG, -bit, UU, ALU.mult, ALU.add, deps=[tg],
                              sig=True)
                trr = vstt(GG, U0, -128.0, UU, ALU.add, ALU.subtract,
                           deps=[tu, tu0], sig=True)
                for g4 in range(8):
                    ttp = None
                    for j in range(4):
                        kc = g4 * 4 + j
                        ttp = tp(PS2[:, 1024 + j * 128:1152 + j * 128],
                                 GG[:, kc * 128:(kc + 1) * 128],
                                 deps=([trr, last_t] if last_t else [trr]),
                                 sig=(j == 3))
                    dst = PH.ap().rearrange("p (c m) -> p c m", m=512)[
                        :, g4 * 4:(g4 + 1) * 4, tb * 128:(tb + 1) * 128]
                    last_t = vcopy(dst, PS2[:, 1024:1536].rearrange(
                        "p (a m) -> p a m", a=4), deps=[ttp], sig=True)
            txq_done = last_t

            G1 = PW.ap()[:, 2048:2560]
            wtok = [None, None]
            ttr = [None, None]
            xconv = [None, None]
            last_ep = [None, None]
            geprev = None
            for ms in range(8):
                psA = PS1 if ms % 2 == 0 else PS2
                tbm = dma("sync", BMAT, bfull[:, ms * 512:(ms + 1) * 512],
                          deps=[geprev] if geprev else [txq_done], ch="BM")
                lastmm = [None] * 4
                for ks in range(32):
                    slot = ks % 2
                    piece = wft[ks * 128:(ks + 1) * 128,
                                ms * 512:(ms + 1) * 512]
                    d = [txq_done] + tth
                    if ttr[slot] is not None:
                        d.append(ttr[slot])
                    tw = dma("sync", WBUF[slot], piece, deps=d,
                         ch=("W0", "W1")[slot])
                    dtern = [tw] + tth
                    if wtok[slot] is not None:
                        dtern.append(wtok[slot])
                    tg1 = vts(G1, WBUF[slot], v_th, ALU.is_ge, deps=dtern,
                              sig=True)
                    tg2 = vts(TSTB[slot], WBUF[slot], v_thn, ALU.is_le,
                              deps=[tw, wtok[slot]], sig=True)
                    tg3 = vtt(TSTB[slot], G1, TSTB[slot], ALU.subtract,
                              deps=[tg1, tg2], sig=True)
                    ttr[slot] = tg3
                    dcv = [txq_done]
                    if xconv[slot] is not None:
                        dcv.append(xconv[slot])
                    tcv = vcopy(XQB[slot],
                                PH.ap()[:, ks * 512:(ks + 1) * 512],
                                deps=dcv, sig=True)
                    lm = None
                    for tb in range(4):
                        dd = [tg3, tcv]
                        if ks == 0:
                            dd.append(last_ep[ms % 2])
                        lm = mm(psA[:, tb * 512:(tb + 1) * 512],
                                XQB[slot][:, tb * 128:(tb + 1) * 128],
                                TSTB[slot], start=(ks == 0), stop=(ks == 31),
                                deps=dd, sig=(tb == 3))
                        lastmm[tb] = lm
                    wtok[slot] = lm
                    xconv[slot] = lm
                tep = None
                for tb in range(4):
                    tyl = vstt(T1A, psA[:, tb * 512:(tb + 1) * 512],
                               svg3[half * 4 + tb], BMAT, ALU.mult, ALU.add,
                               deps=[lastmm[3], tbm, geprev], sig=True)
                    geprev = act(PH3_4096[:, 4 + tb,
                                          ms * 512:(ms + 1) * 512],
                                 T1A, ACTF.Gelu, deps=[tyl], sig=True)
                    tep = geprev
                last_ep[ms % 2] = tep
            last_tr = tep

            tlast = None
            for tb in range(4):
                gt = PH3_4096[:, 4 + tb, :]
                tx1 = dma("sync", X1T,
                          x1_sc[half * 512 + tb * 128:
                                half * 512 + (tb + 1) * 128, :],
                          deps=([last_tr] if tb == 0 else [tlast]), ch="X0")
                tfa = act(FT, X1T, ACTF.Identity, bias=bias3[half * 4 + tb],
                          scale=rstd3[half * 4 + tb], deps=[tx1, last_tr],
                          sig=True)
                tsum = vtt(FT, FT, gt, ALU.add, deps=[tfa, last_tr], sig=True)
                tap, _, _ = layernorm(FT, FT, [tsum])
                tout = vtt(FT, FT, X1T, ALU.add, deps=[tap], sig=True)
                tlast = dma("gpsimd",
                            out[half * 512 + tb * 128:
                                half * 512 + (tb + 1) * 128, :],
                            FT, deps=[tout], ch="ST")
            return tlast

        phase_A(0)
        phase_B(0)
        phase_A(1)
        phase_B(1)
        phase_C(0)
        phase_C(1)
        tth = phase_D0()
        phase_D(0, tth)
        phase_D(1, tth)

        s.op("vector", (lambda e: e.engine_nop()),
             deps=[("S0", s.cnt["S0"]), ("S1", s.cnt["S1"])])

        with nc.Block() as blk:
            def make(engname):
                def run(eng):
                    for wl, emit, tok in s.stream[engname]:
                        for (sm, v) in wl:
                            eng.wait_ge(sem[sm], v)
                        inst = emit(eng)
                        if tok is not None:
                            inst.then_inc(sem[tok[0]],
                                          16 if tok[0] in DMA_CHANNELS else 1)
                return run

            blk.sync(make("sync"))
            blk.gpsimd(make("gpsimd"))
            blk.tensor(make("tensor"))
            blk.vector(make("vector"))
            blk.scalar(make("scalar"))

    return nc


def _host_tables():
    HEAD_DIM = 128
    ROPE_THETA = 10000.0
    inv_freq = 1.0 / ROPE_THETA ** (np.arange(0, HEAD_DIM, 2,
                                              dtype=np.float32) / HEAD_DIM)
    pos = np.arange(S, dtype=np.float32)
    freqs = np.outer(pos, inv_freq)
    emb = np.concatenate([freqs, freqs], axis=-1)
    cos = np.cos(emb).astype(np.float32)
    sin = np.sin(emb).astype(np.float32)
    return cos, sin


def kernel(L0_attn_hidden_states, L0_attn_Wq, L0_attn_Wkr, L0_attn_Wdk,
           L0_attn_Wupk, L0_attn_Wupv, L0_attn_Wo, L0_ffn_W, L0_ffn_b):
    X = np.ascontiguousarray(np.asarray(L0_attn_hidden_states, np.float32))
    WqT = np.ascontiguousarray(np.asarray(L0_attn_Wq, np.float32).T)
    WkrT = np.ascontiguousarray(np.asarray(L0_attn_Wkr, np.float32).T)
    WdkT = np.ascontiguousarray(np.asarray(L0_attn_Wdk, np.float32).T)
    WupkT = np.ascontiguousarray(np.asarray(L0_attn_Wupk, np.float32).T)
    WupvT = np.ascontiguousarray(np.asarray(L0_attn_Wupv, np.float32).T)
    WoT = np.ascontiguousarray(np.asarray(L0_attn_Wo, np.float32).T)
    WfT = np.ascontiguousarray(np.asarray(L0_ffn_W, np.float32).T)
    bfl = np.ascontiguousarray(np.broadcast_to(
        np.asarray(L0_ffn_b, np.float32)[None, :], (128, D)))

    cos, sin = _host_tables()
    ident = np.eye(128, dtype=np.float32)
    ones = np.ones((128, 128), dtype=np.float32)
    diag = np.where(np.arange(128)[None, :] <= np.arange(128)[:, None],
                    np.float32(0.0), np.float32(NEG1E9)).astype(np.float32)
    pqm = np.zeros((128, 128), np.float32)
    for i in range(64):
        pqm[i, 64 + i] = 1.0
        pqm[64 + i, i] = 1.0
    pkm = np.zeros((128, 128), np.float32)
    for base in (0, 64):
        for i in range(32):
            pkm[base + i, base + 32 + i] = 1.0
            pkm[base + 32 + i, base + i] = 1.0

    cosq_rows = np.ascontiguousarray(cos.T)
    sinq_rows = sin.T.copy()
    sinq_rows[0:64] *= -1.0
    kmap = np.zeros(128, dtype=np.int64)
    for r in range(128):
        rr = r % 64
        kmap[r] = rr if rr < 32 else 64 + (rr - 32)
    cosk_rows = np.ascontiguousarray(cos.T[kmap])
    sink_rows = sin.T[kmap].copy()
    for r in range(128):
        if (r % 64) < 32:
            sink_rows[r] *= -1.0

    nc = build_program()
    in_maps = []
    own_blocks_all = []
    for c in range(8):
        b = c // 2
        odd = c % 2
        h0 = [4, 5, 6, 7, 0, 1, 2, 3] if odd else [0, 1, 2, 3, 4, 5, 6, 7]
        h1 = ([8, 9, 10, 11, 12, 13, 14, 15] if odd
              else [12, 13, 14, 15, 8, 9, 10, 11])
        own_blocks_all.append(h0[:4] + h1[:4])
        perm0 = np.concatenate([np.arange(blk * 128, (blk + 1) * 128)
                                for blk in h0])
        perm1 = np.concatenate([np.arange(blk * 128, (blk + 1) * 128)
                                for blk in h1])
        xb = X[b]
        tail = np.zeros((128, 2), np.float32)
        tail[:, 0 if not odd else 1] = np.float32(NEG1E9)
        m = {
            "x_h0": np.ascontiguousarray(xb[perm0]),
            "x_h1": np.ascontiguousarray(xb[perm1]),
            "x_own": np.ascontiguousarray(
                np.concatenate([xb[perm0[:512]], xb[perm1[:512]]])),
            "wqt": WqT, "wkrt": WkrT, "wdkt": WdkT, "wupkt": WupkT,
            "wupvt": WupvT, "wot": WoT, "wft": WfT, "bfull": bfl,
            "identc": ident, "diagm": diag, "pq": pqm, "pk": pkm,
            "onesc": ones, "tailmask": tail,
            "cosq0": np.ascontiguousarray(cosq_rows[:, perm0[:512]]),
            "sinq0": np.ascontiguousarray(sinq_rows[:, perm0[:512]]),
            "cosq1": np.ascontiguousarray(cosq_rows[:, perm1[:512]]),
            "sinq1": np.ascontiguousarray(sinq_rows[:, perm1[:512]]),
            "cosk0": np.ascontiguousarray(cosk_rows[:, perm0]),
            "sink0": np.ascontiguousarray(sink_rows[:, perm0]),
            "cosk1": np.ascontiguousarray(cosk_rows[:, perm1]),
            "sink1": np.ascontiguousarray(sink_rows[:, perm1]),
        }
        in_maps.append(m)

    res = run_bass_kernel_spmd(nc, in_maps, core_ids=list(range(8)))
    full = np.empty((B, S, D), np.float32)
    for c in range(8):
        b = c // 2
        o = np.asarray(res.results[c]["out"]).reshape(8, 128, D)
        for i, blk in enumerate(own_blocks_all[c]):
            full[b, blk * 128:(blk + 1) * 128, :] = o[i]
    return full


# version 07
# speedup vs baseline: 1.1414x; 1.0071x over previous
import numpy as np
from contextlib import ExitStack

import concourse.bass as bass
import concourse.mybir as mybir
from concourse.bass_utils import run_bass_kernel_spmd

F32 = mybir.dt.float32
BF16 = mybir.dt.bfloat16
AXX = mybir.AxisListType.X
ALU = mybir.AluOpType
ACTF = mybir.ActivationFunctionType

B, S, D = 4, 2048, 4096
HALF = 1024
NKV = 8
HD = 128
CKV = 512
RD = 512
INV_SQRT_HD = 0.08838834764831845
NEG1E9 = -1.0e9
EPS = 1.0e-5

ENGINES = ("sync", "gpsimd", "tensor", "vector", "scalar")


DMA_CHANNELS = ("W0", "W1", "X0", "X1", "BM", "CT", "S0", "S1")


class Sched:

    def __init__(self):
        self.cnt = {"T": 0, "V": 0, "A": 0}
        for c in DMA_CHANNELS:
            self.cnt[c] = 0
        self.stream = {e: [] for e in ENGINES}
        self.lastw = {e: {} for e in ENGINES}
        self.own = {"tensor": "T", "vector": "V", "scalar": "A"}

    def op(self, eng, emit, deps=(), sig=False, ch=None):
        waits = {}
        for d in deps:
            if d is None:
                continue
            sm, v = d
            if v <= 0:
                continue
            if sm == "T" and self.own.get(eng) == "T":
                continue
            if v > waits.get(sm, 0):
                waits[sm] = v
        wl = []
        for sm, v in sorted(waits.items()):
            if self.lastw[eng].get(sm, 0) >= v:
                continue
            self.lastw[eng][sm] = v
            wl.append((sm, v))
        tok = None
        if sig:
            sm = ch if ch is not None else self.own[eng]
            amt = 16 if ch is not None else 1
            self.cnt[sm] += amt
            tok = (sm, self.cnt[sm])
        self.stream[eng].append([wl, emit, tok])
        return tok

    def retro_sig(self, eng):
        st = self.stream[eng]
        if not st:
            return None
        if st[-1][2] is None:
            sm = self.own[eng]
            self.cnt[sm] += 1
            st[-1][2] = (sm, self.cnt[sm])
        return st[-1][2]

    def join(self):
        toks = [t for t in (self.retro_sig(e)
                            for e in ("tensor", "vector", "scalar")) if t]
        for c in DMA_CHANNELS:
            if self.cnt[c] > 0:
                toks.append((c, self.cnt[c]))
        return toks


def build_program():
    nc = bass.Bass("TRN2", target_bir_lowering=False)

    def din(name, shape):
        return nc.dram_tensor(name, list(shape), F32, kind="ExternalInput").ap()

    xh = [din("x_h0", (HALF, D)), din("x_h1", (HALF, D))]
    x_own = din("x_own", (HALF, D))
    wqt = din("wqt", (D, D))
    wkrt = din("wkrt", (D, RD))
    wdkt = din("wdkt", (D, CKV))
    wupkt = din("wupkt", (CKV, CKV))
    wupvt = din("wupvt", (CKV, NKV * HD))
    wot = din("wot", (D, D))
    wft = din("wft", (D, D))
    bfull = din("bfull", (128, D))
    identc = din("identc", (128, 128))
    diagm = din("diagm", (128, 128))
    pqc = din("pq", (128, 128))
    pkc = din("pk", (128, 128))
    onesc = din("onesc", (128, 128))
    tailc = din("tailmask", (128, 2))
    cosq_in = [din("cosq0", (128, 512)), din("cosq1", (128, 512))]
    sinq_in = [din("sinq0", (128, 512)), din("sinq1", (128, 512))]
    cosk_in = [din("cosk0", (128, HALF)), din("cosk1", (128, HALF))]
    sink_in = [din("sink0", (128, HALF)), din("sink1", (128, HALF))]

    out = nc.dram_tensor("out", [HALF, D], F32, kind="ExternalOutput").ap()

    q_sc = nc.dram_tensor("q_sc", [D, HALF], F32, kind="Internal").ap()
    k_sc = nc.dram_tensor("k_sc", [NKV, 128, S], F32, kind="Internal").ap()
    v_sc = nc.dram_tensor("v_sc", [NKV, S, HD], F32, kind="Internal").ap()
    ao_sc = nc.dram_tensor("ao_sc", [D, HALF], F32, kind="Internal").ap()
    x1_sc = nc.dram_tensor("x1_sc", [HALF, D], F32, kind="Internal").ap()

    with ExitStack() as ctx:
        def sbuf(name, shape, dtype=F32):
            return ctx.enter_context(nc.sbuf_tensor(name, shape, dtype))

        PH = sbuf("PH", [128, 32768])
        PW = sbuf("PW", [128, 5632])
        PX = sbuf("PX", [128, 8192])
        PC = sbuf("PC", [128, 3712])
        PV = sbuf("PV", [128, 512])
        PB = sbuf("PB", [128, 4096], BF16)
        sem = {x: ctx.enter_context(nc.semaphore("sem_" + x))
               for x in (("T", "V", "A") + DMA_CHANNELS)}
        PS1 = ctx.enter_context(nc.psum_tensor("PS1", [128, 2048], F32))
        PS2 = ctx.enter_context(nc.psum_tensor("PS2", [128, 2048], F32))

        IDENT = PC.ap()[:, 0:128]
        DIAG = PC.ap()[:, 128:256]
        PQM = PC.ap()[:, 256:384]
        PKM = PC.ap()[:, 384:512]
        ONES = PC.ap()[:, 512:640]
        COSQ = PC.ap()[:, 640:1152]
        SINQ = PC.ap()[:, 1152:1664]
        COSK = PC.ap()[:, 1664:2688]
        SINKD = PC.ap()[:, 2688:3712]

        s = Sched()
        V1 = PV.ap()

        class Vec:
            def __init__(self):
                self.n = 0

            def slot(self, w=1):
                a = self.n
                self.n += w
                assert self.n <= 490, "vec pool overflow"
                return V1[:, a:a + w]

        vec = Vec()
        v_zero = vec.slot()
        v_tails = vec.slot(2)
        v_eps = vec.slot()
        v_1285 = vec.slot()
        v_acc = vec.slot()
        v_gmc = vec.slot()
        v_th = vec.slot()
        v_thn = vec.slot()
        rstd3 = [vec.slot() for _ in range(8)]
        bias3 = [vec.slot() for _ in range(8)]
        svg3 = [vec.slot() for _ in range(8)]
        v_m = V1[:, 494:495]
        v_sm = V1[:, 495:496]
        v_bi = V1[:, 496:497]
        v_rc = V1[:, 497:498]
        v_r3 = V1[:, 498:499]
        v_am = V1[:, 499:500]
        v_rcd = V1[:, 500:501]
        v_ssa = [V1[:, 501 + i:502 + i] for i in range(4)]
        va_d0 = V1[:, 505:506]

        st_rr = [0]

        def dma(queue, dst, src, deps=(), ch="CT"):
            if ch == "ST":
                ch = ("S0", "S1")[st_rr[0] % 2]
                st_rr[0] += 1
            deps = list(deps) + [(ch, s.cnt[ch])]
            return s.op(queue,
                        (lambda e, d=dst, so=src: e.dma_start(out=d, in_=so)),
                        deps, sig=True, ch=ch)

        def mm(outap, lhsT, rhs, start, stop, deps=(), sig=False):
            return s.op("tensor",
                        (lambda e, o=outap, l=lhsT, r=rhs, st=start, sp=stop:
                         e.matmul(o, l, r, start=st, stop=sp)), deps, sig=sig)

        def tp(outap, inap, deps=(), sig=False):
            return s.op("tensor",
                        (lambda e, o=outap, i=inap: e.transpose(o, i, IDENT)),
                        deps, sig=sig)

        def vcopy(dst, src, deps=(), sig=False):
            return s.op("vector",
                        (lambda e, d=dst, so=src: e.tensor_copy(d, so)),
                        deps, sig=sig)

        def vtt(dst, a, bb, op, deps=(), sig=False):
            return s.op("vector",
                        (lambda e, d=dst, x=a, y=bb, o=op:
                         e.tensor_tensor(d, x, y, op=o)), deps, sig=sig)

        def vts(dst, a, s1, op, deps=(), sig=False):
            return s.op("vector",
                        (lambda e, d=dst, x=a, c=s1, o=op:
                         e.tensor_single_scalar(d, x, c, op=o)), deps, sig=sig)

        def vstt(dst, in0, sc, in1, op0, op1, deps=(), sig=False):
            return s.op("vector",
                        (lambda e, d=dst, a=in0, c=sc, bb=in1, o0=op0, o1=op1:
                         e.scalar_tensor_tensor(d, a, c, bb, op0=o0, op1=o1)),
                        deps, sig=sig)

        def vred(dst, src, op, deps=(), sig=False, absv=False, neg=False):
            return s.op("vector",
                        (lambda e, d=dst, so=src, o=op, ab=absv, ng=neg:
                         e.tensor_reduce(d, so, axis=AXX, op=o,
                                         apply_absolute_value=ab, negate=ng)),
                        deps, sig=sig)

        def vmemset(dst, val, deps=(), sig=False):
            return s.op("vector", (lambda e, d=dst, v=val: e.memset(d, v)),
                        deps, sig=sig)

        def vrecip(dst, src, deps=(), sig=False):
            return s.op("vector",
                        (lambda e, d=dst, so=src: e.reciprocal(d, so)),
                        deps, sig=sig)

        def act(dst, src, func, bias=None, scale=1.0, accum=None, deps=(),
                sig=False):
            bias_ap = v_zero if bias is None else bias

            def emitf(e, d=dst, so=src, f=func, bb=bias_ap, sc=scale, au=accum):
                if au is None:
                    return e.activation(d, so, f, bias=bb, scale=sc)
                return e.activation(d, so, f, bias=bb, scale=sc, accum_out=au)
            return s.op("scalar", emitf, deps, sig=sig)

        dma("sync", IDENT, identc)
        dma("sync", DIAG, diagm)
        dma("sync", PQM, pqc)
        dma("sync", PKM, pkc)
        dma("sync", ONES, onesc)
        dma("sync", v_tails, tailc)
        vmemset(v_zero, 0.0)
        vmemset(v_eps, EPS)
        vmemset(v_1285, 128.5)
        tconst = s.join()

        sq_ps = PS2[:, 0:1024]

        def layernorm(xt, dst, ready, rstd=None, biasv=None,
                      sq_region=None):
            v_sum = vec.slot()
            vv = [vec.slot() for _ in range(4)]
            rst = rstd if rstd is not None else vec.slot()
            bia = biasv if biasv is not None else vec.slot()
            sq = sq_ps if sq_region is None else sq_region
            tsum = vred(v_sum, xt, ALU.add, deps=ready, sig=True)
            tsq = None
            for i in range(4):
                tsq = act(sq, xt[:, i * 1024:(i + 1) * 1024], ACTF.Square,
                          accum=vv[i], deps=ready, sig=(i == 3))
            c1 = vtt(vv[0], vv[0], vv[1], ALU.add, deps=[tsq], sig=True)
            c2 = vtt(vv[2], vv[2], vv[3], ALU.add, deps=[tsq], sig=True)
            c3 = vtt(vv[0], vv[0], vv[2], ALU.add, deps=[c1, c2], sig=True)
            c4 = vts(vv[0], vv[0], 1.0 / 4096.0, ALU.mult, deps=[c3],
                     sig=True)
            c5 = vts(v_sum, v_sum, 1.0 / 4096.0, ALU.mult, deps=[tsum],
                     sig=True)
            c6 = vtt(vv[1], v_sum, v_sum, ALU.mult, deps=[c5], sig=True)
            tvar = vtt(vv[0], vv[0], vv[1], ALU.subtract, deps=[c4, c6],
                       sig=True)
            tsd = act(rst, vv[0], ACTF.Sqrt, bias=v_eps, deps=[tvar],
                      sig=True)
            trst = vrecip(rst, rst, deps=[tsd], sig=True)
            tb = vstt(bia, v_sum, -1.0, rst, ALU.mult, ALU.mult,
                      deps=[trst, c5], sig=True)
            tap = act(dst, xt, ACTF.Identity, bias=bia, scale=rst, deps=[tb],
                      sig=True)
            return tap, rst, bia

        WBUF = [PW.ap()[:, 0:512], PW.ap()[:, 512:1024]]
        STAGE = [PW.ap()[:, 1024:5120], PX.ap()[:, 0:4096]]
        T1A = PW.ap()[:, 5120:5632]
        XTbuf = [PX.ap()[:, 0:4096], PX.ap()[:, 4096:8192]]
        CKVB = PX.ap()[:, 4096:8192]
        PH3_1024 = PH.ap().rearrange("p (c m) -> p c m", m=1024)
        PH3_4096 = PH.ap().rearrange("p (c m) -> p c m", m=4096)
        TSTB = [PB.ap()[:, 0:512], PB.ap()[:, 512:1024]]
        XQB = [PB.ap()[:, 1024:1536], PB.ap()[:, 1536:2048]]
        WHs = [PB.ap()[:, 0:512], PB.ap()[:, 512:1024]]
        WLs = [PB.ap()[:, 1024:1536], PB.ap()[:, 1536:2048]]
        XHs = [PB.ap()[:, 2048:2560], PB.ap()[:, 2560:3072]]
        XLs = [PB.ap()[:, 3072:3584], PB.ap()[:, 3584:4096]]

        def hcol(kc):
            return PH.ap()[:, kc * 1024:(kc + 1) * 1024]

        def phase_A(half):
            joinA = s.join()
            trope = [dma("sync", COSQ, cosq_in[half], deps=joinA),
                     dma("sync", SINQ, sinq_in[half]),
                     dma("sync", COSK, cosk_in[half]),
                     dma("sync", SINKD, sink_in[half])] + tconst

            h_done = []
            tile_tp = [None, None]
            for t in range(8):
                xt = XTbuf[t % 2]
                d = list(joinA)
                if tile_tp[t % 2] is not None:
                    d.append(tile_tp[t % 2])
                txl = dma("sync", xt, xh[half][t * 128:(t + 1) * 128, :],
                          deps=d, ch=("X0", "X1")[t % 2])
                tap, _, _ = layernorm(xt, xt, [txl])
                for g4 in range(8):
                    ttp = None
                    for j in range(4):
                        kc = g4 * 4 + j
                        ttp = tp(PS2[:, 1024 + j * 128:1152 + j * 128],
                                 xt[:, kc * 128:(kc + 1) * 128],
                                 deps=([tap, h_done[-1]] if h_done else [tap]),
                                 sig=(j == 3))
                    dst = PH3_1024[:, g4 * 4:(g4 + 1) * 4,
                                   t * 128:(t + 1) * 128]
                    tev = vcopy(dst, PS2[:, 1024:1536].rearrange(
                        "p (a m) -> p a m", a=4), deps=[ttp], sig=True)
                    h_done.append(tev)
                tile_tp[t % 2] = ttp
            th = h_done[-1]

            def proj(wt_ap, Kdim, Mout, rhs_fn, sink_fn, wdeps,
                     init_sink=None):
                nks = Kdim // 128
                wtok = [None, None]
                wsub = [None, None]
                xtok = [None, None]
                ev1_prev, ev2_prev = None, None
                ps2rel_prev = None
                sink_tok = [init_sink, init_sink]
                last_sink = init_sink
                for ms in range(Mout // 512):
                    stg = STAGE[ms % 2]
                    for ks in range(nks):
                        slot = ks % 2
                        piece = wt_ap[ks * 128:(ks + 1) * 128,
                                      ms * 512:(ms + 1) * 512]
                        d = list(wdeps)
                        if wsub[slot] is not None:
                            d.append(wsub[slot])
                        tw = dma("sync", WBUF[slot], piece, deps=d,
                                 ch=("W0", "W1")[slot])
                        twh = act(WHs[slot], WBUF[slot], ACTF.Identity,
                                  deps=[tw, wtok[slot]], sig=True)
                        twl = vtt(WLs[slot], WBUF[slot], WHs[slot],
                                  ALU.subtract, deps=[twh], sig=True)
                        wsub[slot] = twl
                        cx = [None, None]
                        for th_i in range(2):
                            xi = th_i
                            r = rhs_fn(ks, th_i)
                            c1 = act(XHs[xi], r, ACTF.Identity,
                                     deps=list(wdeps) + [xtok[xi]],
                                     sig=True)
                            cx[th_i] = vtt(XLs[xi], r, XHs[xi],
                                           ALU.subtract, deps=[c1], sig=True)
                        lastmm = None
                        for th_i in range(2):
                            xi = th_i
                            psb = PS1 if th_i == 0 else PS2
                            for mi in range(4):
                                for pi, (lw, rx) in enumerate(
                                        ((WHs[slot], XHs[xi]),
                                         (WHs[slot], XLs[xi]),
                                         (WLs[slot], XHs[xi]))):
                                    dd = [twl, cx[th_i]]
                                    if ks == 0 and pi == 0:
                                        dd.append(ev1_prev if th_i == 0
                                                  else ev2_prev)
                                        if th_i == 1:
                                            dd.append(ps2rel_prev)
                                    lastmm = mm(
                                        psb[:, mi * 512:(mi + 1) * 512],
                                        lw[:, mi * 128:(mi + 1) * 128], rx,
                                        start=(ks == 0 and pi == 0),
                                        stop=(ks == nks - 1 and pi == 2),
                                        deps=dd,
                                        sig=(mi == 3 and pi == 2))
                            xtok[xi] = lastmm
                        wtok[slot] = lastmm
                    mmtok = wtok[(nks - 1) % 2]
                    sdep = sink_tok[ms % 2]
                    ev1 = None
                    for half2 in range(2):
                        ev1 = act(
                            stg.rearrange("p (m t) -> p m t", t=1024)
                            [:, half2 * 2:half2 * 2 + 2, 0:512],
                            PS1[:, half2 * 1024:(half2 + 1) * 1024]
                            .rearrange("p (m t) -> p m t", t=512),
                            ACTF.Identity, deps=[mmtok, sdep], sig=True)
                    ev2 = None
                    for half2 in range(2):
                        ev2 = vcopy(
                            stg.rearrange("p (m t) -> p m t", t=1024)
                            [:, half2 * 2:half2 * 2 + 2, 512:1024],
                            PS2[:, half2 * 1024:(half2 + 1) * 1024]
                            .rearrange("p (m t) -> p m t", t=512),
                            deps=[mmtok, sdep], sig=True)
                    ev1_prev, ev2_prev = ev1, ev2
                    evm = s.op("vector", (lambda e: e.engine_nop()),
                               deps=[ev1, ev2], sig=True)
                    last_sink, ps2rel_prev = sink_fn(ms, evm, stg,
                                                     ps2rel_prev)
                    sink_tok[ms % 2] = last_sink
                return last_sink

            def rhs_h(ks, th_i):
                return hcol(ks)[:, th_i * 512:(th_i + 1) * 512]

            def rhs_ckv(ks, th_i):
                return CKVB[:, ks * 1024 + th_i * 512:
                            ks * 1024 + th_i * 512 + 512]

            def sink_kr(ms, tev, stg, ps2rel):
                tend = tev
                prev_tro = [ps2rel, ps2rel]
                for mi in range(4):
                    ch = stg[:, mi * 1024:(mi + 1) * 1024]
                    tros = []
                    for th_i in range(2):
                        chh = ch[:, th_i * 512:(th_i + 1) * 512]
                        dd = [tev] + trope
                        if prev_tro[th_i] is not None:
                            dd.append(prev_tro[th_i])
                        tsw = mm(PS2[:, 1024 + th_i * 512:1536 + th_i * 512],
                                 PKM, chh, start=True, stop=True, deps=dd,
                                 sig=True)
                        t1 = vtt(T1A, chh,
                                 COSK[:, th_i * 512:(th_i + 1) * 512],
                                 ALU.mult, deps=[tev] + trope, sig=True)
                        t2 = vtt(PS2[:, 1024 + th_i * 512:1536 + th_i * 512],
                                 PS2[:, 1024 + th_i * 512:1536 + th_i * 512],
                                 SINKD[:, th_i * 512:(th_i + 1) * 512],
                                 ALU.mult, deps=[tsw], sig=True)
                        tro = vtt(chh, T1A,
                                  PS2[:, 1024 + th_i * 512:1536 + th_i * 512],
                                  ALU.add, deps=[t1, t2], sig=True)
                        prev_tro[th_i] = tro
                        tros.append(tro)
                    ka, kb = 2 * mi, 2 * mi + 1
                    for (r0, kv, dr) in ((0, ka, 0), (32, ka, 64),
                                         (64, kb, 0), (96, kb, 64)):
                        tend = dma("gpsimd",
                                   k_sc[kv, dr:dr + 32,
                                        half * HALF:(half + 1) * HALF],
                                   ch[r0:r0 + 32, :], deps=tros, ch="ST")
                return tend, prev_tro[1]

            tkr = proj(wkrt, D, RD, rhs_h, sink_kr, [th])

            def sink_dk(ms, tev, stg, ps2rel):
                t = None
                for mi in range(4):
                    t = vcopy(CKVB[:, mi * 1024:(mi + 1) * 1024],
                              stg[:, mi * 1024:(mi + 1) * 1024],
                              deps=[tev], sig=(mi == 3))
                return t, ps2rel

            tdk = proj(wdkt, D, CKV, rhs_h, sink_dk, [th, tkr],
                       init_sink=tkr)

            def sink_upk(ms, tev, stg, ps2rel):
                tend = tev
                for mi in range(4):
                    ch = stg[:, mi * 1024:(mi + 1) * 1024]
                    ka, kb = 2 * mi, 2 * mi + 1
                    for (r0, kv, dr) in ((0, ka, 32), (32, ka, 96),
                                         (64, kb, 32), (96, kb, 96)):
                        tend = dma("gpsimd",
                                   k_sc[kv, dr:dr + 32,
                                        half * HALF:(half + 1) * HALF],
                                   ch[r0:r0 + 32, :], deps=[tev], ch="ST")
                return tend, ps2rel

            tupk = proj(wupkt, CKV, CKV, rhs_ckv, sink_upk, [tdk],
                        init_sink=tdk)

            def sink_upv(ms, tev, stg, ps2rel):
                tend = tev
                ttp_prev = ps2rel
                for mi in range(4):
                    kv = ms * 4 + mi
                    ch = stg[:, mi * 1024:(mi + 1) * 1024]
                    for tb4 in range(2):
                        ttp = None
                        for j in range(4):
                            tb = tb4 * 4 + j
                            dd = [tev]
                            if ttp_prev is not None:
                                dd.append(ttp_prev)
                            ttp = tp(PS2[:, 1024 + j * 128:1152 + j * 128],
                                     ch[:, tb * 128:(tb + 1) * 128],
                                     deps=dd, sig=(j == 3))
                        tvv = vcopy(T1A, PS2[:, 1024:1536],
                                    deps=[ttp, tend], sig=True)
                        ttp_prev = tvv
                        dst = v_sc[kv, half * HALF + tb4 * 512:
                                   half * HALF + (tb4 + 1) * 512, :]
                        tend = dma("gpsimd",
                                   dst.rearrange("(j p) d -> p j d", p=128),
                                   T1A.rearrange("p (j d) -> p j d", j=4),
                                   deps=[tvv], ch="ST")
                return tend, ttp_prev

            tupv = proj(wupvt, CKV, NKV * HD, rhs_ckv, sink_upv, [tupk],
                        init_sink=tupk)

            def projq(wdeps, init_sink):
                wtok = [None, None]
                wsub = [None, None]
                xtok = [None, None]
                ev_prev = None
                sink_tok = [init_sink, init_sink]
                last_sink = init_sink
                prev_tro = None
                for ms in range(8):
                    stg = STAGE[ms % 2]
                    for ks in range(32):
                        slot = ks % 2
                        piece = wqt[ks * 128:(ks + 1) * 128,
                                    ms * 512:(ms + 1) * 512]
                        d = list(wdeps)
                        if wsub[slot] is not None:
                            d.append(wsub[slot])
                        tw = dma("sync", WBUF[slot], piece, deps=d,
                                 ch=("W0", "W1")[slot])
                        twh = act(WHs[slot], WBUF[slot], ACTF.Identity,
                                  deps=[tw, wtok[slot]], sig=True)
                        twl = vtt(WLs[slot], WBUF[slot], WHs[slot],
                                  ALU.subtract, deps=[twh], sig=True)
                        wsub[slot] = twl
                        r = hcol(ks)[:, 0:512]
                        c1 = act(XHs[slot], r, ACTF.Identity,
                                 deps=list(wdeps) + [xtok[slot]], sig=True)
                        cx = vtt(XLs[slot], r, XHs[slot], ALU.subtract,
                                 deps=[c1], sig=True)
                        lastmm = None
                        for mi in range(4):
                            for pi, (lw, rx) in enumerate(
                                    ((WHs[slot], XHs[slot]),
                                     (WHs[slot], XLs[slot]),
                                     (WLs[slot], XHs[slot]))):
                                dd = [twl, cx]
                                if ks == 0 and pi == 0:
                                    dd.append(ev_prev)
                                lastmm = mm(PS1[:, mi * 512:(mi + 1) * 512],
                                            lw[:, mi * 128:(mi + 1) * 128],
                                            rx,
                                            start=(ks == 0 and pi == 0),
                                            stop=(ks == 31 and pi == 2),
                                            deps=dd,
                                            sig=(mi == 3 and pi == 2))
                        wtok[slot] = lastmm
                        xtok[slot] = lastmm
                    ev = None
                    for half2 in range(2):
                        ev = vcopy(
                            stg.rearrange("p (m t) -> p m t", t=512)
                            [:, half2 * 2:half2 * 2 + 2, :],
                            PS1[:, half2 * 1024:(half2 + 1) * 1024]
                            .rearrange("p (m t) -> p m t", t=512),
                            deps=[wtok[1], sink_tok[ms % 2]], sig=True)
                    ev_prev = ev
                    tend = ev
                    for mi in range(4):
                        g = ms * 4 + mi
                        ch = stg[:, mi * 512:(mi + 1) * 512]
                        dd = [ev] + trope
                        if prev_tro is not None:
                            dd.append(prev_tro)
                        tsw = mm(PS2[:, 1024:1536], PQM, ch, start=True,
                                 stop=True, deps=dd, sig=True)
                        t1 = vtt(T1A, ch, COSQ, ALU.mult,
                                 deps=[ev] + trope, sig=True)
                        t2 = vtt(PS2[:, 1024:1536], PS2[:, 1024:1536], SINQ,
                                 ALU.mult, deps=[tsw], sig=True)
                        prev_tro = vtt(ch, T1A, PS2[:, 1024:1536], ALU.add,
                                       deps=[t1, t2], sig=True)
                        tend = dma("gpsimd",
                                   q_sc[g * 128:(g + 1) * 128,
                                        half * 512:(half + 1) * 512],
                                   ch, deps=[prev_tro], ch="ST")
                    last_sink = tend
                    sink_tok[ms % 2] = last_sink
                return last_sink

            return projq([th, tupv], tupv)

        KBUF = PX.ap()[:, 0:2048]
        VBUF = PX.ap()[:, 2048:4096]
        QBUF = PX.ap()[:, 4096:6144]
        PSB = PW.ap()[:, 0:2048]
        PTSB = [PW.ap()[:, 2048:2176], PW.ap()[:, 2176:2304]]
        AOTM = PW.ap()[:, 2304:2432]
        AOST = PW.ap()[:, 2432:2944]

        def phase_B(p):
            joinB = s.join()
            TAILP = v_tails[:, p:p + 1].broadcast_to([128, 512])
            Lmax = 1024 if p == 0 else 2048
            lastk = joinB
            prev_texp = None
            prev_tmx = None
            prev_tao = None
            prev_tev2 = None
            for kv in range(NKV):
                dk = lastk if isinstance(lastk, list) else [lastk]
                tk = dma("sync", KBUF[:, 0:Lmax], k_sc[kv, :, 0:Lmax], deps=dk,
                         ch="W0")
                tv = dma("sync",
                         VBUF[:, 0:Lmax].rearrange("p (j d) -> p j d", d=128),
                         v_sc[kv, 0:Lmax, :].rearrange("(j p) d -> p j d",
                                                       p=128), deps=dk,
                         ch="W1")
                tq = dma("sync",
                         QBUF.rearrange("p (h t) -> p h t", h=4),
                         q_sc[kv * 512:(kv + 1) * 512,
                              p * 512:(p + 1) * 512].rearrange(
                                  "(h p) t -> p h t", p=128), deps=dk,
                         ch="X0")
                slot_last = []
                for sl in range(4):
                    sidx = p * 4 + sl
                    if p == 0:
                        part1 = list(range(sl + 1))
                        part2 = [4, 5, 6, 7]
                        diagpos = sl
                    else:
                        part1 = list(range(5 + sidx))
                        part2 = [12, 13, 14, 15]
                        diagpos = 4 + sidx
                    blocks = part1 + part2
                    nblk = len(blocks)
                    L = nblk * 128
                    head_last = []
                    for hh in range(4):
                        qt = QBUF[:, hh * 512 + sl * 128:
                                  hh * 512 + (sl + 1) * 128]
                        tsc = None
                        runs = []
                        off = 0
                        n1 = len(part1)
                        for c0 in range(0, n1, 4):
                            w = min(4, n1 - c0)
                            runs.append((off, part1[c0], w))
                            off += w
                        po2, b02, w2 = off, part2[0], 4
                        while w2 > 0:
                            w1 = min(w2, 4 - (po2 % 4))
                            runs.append((po2, b02, w1))
                            po2 += w1
                            b02 += w1
                            w2 -= w1
                        for (po, b0, w) in runs:
                            dd = [tk, tq]
                            if prev_texp is not None:
                                dd += [prev_texp, prev_tmx]
                            tsc = mm(PS1[:, po * 128:(po + w) * 128], qt,
                                     KBUF[:, b0 * 128:(b0 + w) * 128],
                                     start=True, stop=True, deps=dd,
                                     sig=(po + w == nblk))
                        tmk1 = vtt(PS1[:, diagpos * 128:(diagpos + 1) * 128],
                                   PS1[:, diagpos * 128:(diagpos + 1) * 128],
                                   DIAG, ALU.add, deps=[tsc], sig=True)
                        tmk2 = vtt(PS1[:, L - 512:L], PS1[:, L - 512:L],
                                   TAILP, ALU.add, deps=[tsc], sig=True)
                        prev_tmx = vred(v_m, PS1[:, 0:L], ALU.max,
                                        deps=[tsc, tmk1, tmk2], neg=True,
                                        sig=True)
                        tbb = vts(v_bi, v_m, INV_SQRT_HD, ALU.mult,
                                  deps=[prev_texp, prev_tmx], sig=True)
                        texp = act(PSB[:, 0:L], PS1[:, 0:L], ACTF.Exp,
                                   bias=v_bi, scale=INV_SQRT_HD, accum=v_sm,
                                   deps=[tbb, tsc, prev_tmx], sig=True)
                        prev_texp = texp
                        trc = vrecip(v_rc, v_sm, deps=[texp, prev_tao],
                                     sig=True)
                        tmmv = None
                        slot_busy = [None, None]
                        for j, blk in enumerate(blocks):
                            pslot = j % 2
                            dtp = [texp]
                            if slot_busy[pslot] is not None:
                                dtp.append(slot_busy[pslot])
                            ttp = tp(PS2[:, pslot * 128:(pslot + 1) * 128],
                                     PSB[:, j * 128:(j + 1) * 128],
                                     deps=dtp, sig=True)
                            if pslot == 0:
                                tevp = vcopy(
                                    PTSB[pslot],
                                    PS2[:, pslot * 128:(pslot + 1) * 128],
                                    deps=[ttp], sig=True)
                            else:
                                tevp = act(
                                    PTSB[pslot],
                                    PS2[:, pslot * 128:(pslot + 1) * 128],
                                    ACTF.Identity, deps=[ttp], sig=True)
                            tmmv = mm(PS2[:, 512:640], PTSB[pslot],
                                      VBUF[:, blk * 128:(blk + 1) * 128],
                                      start=(j == 0), stop=(j == nblk - 1),
                                      deps=[tevp, tv], sig=(j == nblk - 1))
                            slot_busy[pslot] = tevp
                        tao = act(AOTM, PS2[:, 512:640], ACTF.Identity,
                                  scale=v_rc, deps=[tmmv, trc], sig=True)
                        prev_tao = tao
                        dd = [tao]
                        if prev_tev2 is not None:
                            dd.append(prev_tev2)
                        ttf = tp(PS2[:, 1024:1152], AOTM, deps=dd, sig=True)
                        dd = [ttf]
                        if len(slot_last) >= 1:
                            dd.append(slot_last[-1])
                        tev2 = vcopy(AOST[:, hh * 128:(hh + 1) * 128],
                                     PS2[:, 1024:1152], deps=dd, sig=True)
                        prev_tev2 = tev2
                        head_last.append(tev2)
                    dst = ao_sc[kv * 512:(kv + 1) * 512,
                                sidx * 128:(sidx + 1) * 128]
                    tst = dma("gpsimd",
                              dst.rearrange("(h p) c -> p h c", p=128),
                              AOST.rearrange("p (h c) -> p h c", h=4),
                              deps=head_last, ch="ST")
                    slot_last.append(tst)
                lastk = slot_last[-1]
            return lastk

        def phase_C(half):
            joinC = s.join()
            tao = dma("sync",
                      PH.ap().rearrange("p (c m) -> p c m", m=512)
                      [:, 0:32, :],
                      ao_sc.rearrange("(c p) m -> p c m", p=128)
                      [:, :, half * 512:(half + 1) * 512],
                      deps=joinC, ch="X0")
            YSTG = [PW.ap()[:, 1024:3072], PW.ap()[:, 3072:5120]]
            wtok = [None, None]
            ev_prev = [None, None]
            tr_prev = [None, None]
            wsub = [None, None]
            xtok = [None, None]
            last_tr = None
            for ms in range(8):
                psA = PS1 if ms % 2 == 0 else PS2
                psB = PS2 if ms % 2 == 0 else PS1
                lastmm = None
                for ks in range(32):
                    slot = ks % 2
                    piece = wot[ks * 128:(ks + 1) * 128,
                                ms * 512:(ms + 1) * 512]
                    d = [tao]
                    if wsub[slot] is not None:
                        d.append(wsub[slot])
                    tw = dma("sync", WBUF[slot], piece, deps=d,
                         ch=("W0", "W1")[slot])
                    twh = act(WHs[slot], WBUF[slot], ACTF.Identity,
                              deps=[tw, wtok[slot]], sig=True)
                    twl = vtt(WLs[slot], WBUF[slot], WHs[slot],
                              ALU.subtract, deps=[twh], sig=True)
                    wsub[slot] = twl
                    r = PH.ap()[:, ks * 512:(ks + 1) * 512]
                    c1 = act(XHs[slot], r, ACTF.Identity,
                             deps=[tao, xtok[slot]], sig=True)
                    cx = vtt(XLs[slot], r, XHs[slot], ALU.subtract,
                             deps=[c1], sig=True)
                    for mi in range(4):
                        for pi, (lw, rx) in enumerate(
                                ((WHs[slot], XHs[slot]),
                                 (WHs[slot], XLs[slot]),
                                 (WLs[slot], XHs[slot]))):
                            dd = [twl, cx]
                            if ks == 0 and pi == 0:
                                dd.append(ev_prev[ms % 2])
                                if ms >= 1:
                                    dd.append(tr_prev[(ms - 1) % 2])
                            lastmm = mm(psA[:, mi * 512:(mi + 1) * 512],
                                        lw[:, mi * 128:(mi + 1) * 128], rx,
                                        start=(ks == 0 and pi == 0),
                                        stop=(ks == 31 and pi == 2),
                                        deps=dd, sig=(mi == 3 and pi == 2))
                    wtok[slot] = lastmm
                    xtok[slot] = lastmm
                tev0 = act(
                    YSTG[ms % 2].rearrange("p (m t) -> p m t", t=512)
                    [:, 0:2, :],
                    psA[:, 0:1024].rearrange("p (m t) -> p m t", t=512),
                    ACTF.Identity, deps=[lastmm, tr_prev[ms % 2]], sig=True)
                tev1 = vcopy(
                    YSTG[ms % 2].rearrange("p (m t) -> p m t", t=512)
                    [:, 2:4, :],
                    psA[:, 1024:2048].rearrange("p (m t) -> p m t", t=512),
                    deps=[lastmm, tr_prev[ms % 2]], sig=True)
                tev = s.op("vector", (lambda e: e.engine_nop()),
                           deps=[tev0, tev1], sig=True)
                ev_prev[ms % 2] = tev
                for tb in range(4):
                    ttp = None
                    for mi in range(4):
                        dd = [tev]
                        if last_tr is not None:
                            dd.append(last_tr)
                        ttp = tp(psB[:, 1024 + mi * 128:1152 + mi * 128],
                                 YSTG[ms % 2][:, mi * 512 + tb * 128:
                                              mi * 512 + (tb + 1) * 128],
                                 deps=dd, sig=(mi == 3))
                    last_tr = vcopy(
                        PH3_4096[:, 4 + tb, ms * 512:(ms + 1) * 512],
                        psB[:, 1024:1536], deps=[ttp], sig=True)
                tr_prev[ms % 2] = last_tr
            tlast = None
            for tb in range(4):
                yt = PH3_4096[:, 4 + tb, :]
                txl = dma("sync", XTbuf[0],
                          x_own[half * 512 + tb * 128:
                                half * 512 + (tb + 1) * 128, :],
                          deps=[tlast] if tlast else [last_tr], ch="X1")
                tap, _, _ = layernorm(yt, yt, [last_tr, txl],
                                      sq_region=PS1[:, 0:1024])
                tadd = vtt(yt, yt, XTbuf[0], ALU.add, deps=[tap, txl],
                           sig=True)
                tlast = dma("gpsimd",
                            x1_sc[half * 512 + tb * 128:
                                  half * 512 + (tb + 1) * 128, :],
                            yt, deps=[tadd], ch="ST")
            return tlast

        def phase_D0():
            joinD = s.join()
            tm = vmemset(v_acc, 0.0, deps=joinD, sig=True)
            WCH = [PW.ap()[:, 1024:5120], PX.ap()[:, 0:4096]]
            tred = [tm, tm]
            tacc = tm
            for ks in range(32):
                slot = ks % 2
                tw = dma("sync", WCH[slot], wft[ks * 128:(ks + 1) * 128, :],
                         deps=(joinD + [tred[slot]] if ks >= 2
                               else joinD), ch=("W0", "W1")[slot])
                t1 = vred(va_d0, WCH[slot], ALU.add, deps=[tw, tacc],
                          absv=True, sig=True)
                tred[slot] = t1
                tacc = vtt(v_acc, v_acc, va_d0, ALU.add, deps=[t1, tacc],
                           sig=True)
            trep = mm(PS2[:, 0:1], ONES, v_acc, start=True, stop=True,
                      deps=[tacc], sig=True)
            tg0 = vts(v_gmc, PS2[:, 0:1], 1.0 / 16777216.0, ALU.mult,
                      deps=[trep], sig=True)
            tgm = vts(v_gmc, v_gmc, EPS, ALU.max, deps=[tg0], sig=True)
            tth = vts(v_th, v_gmc, 0.5, ALU.mult, deps=[tgm], sig=True)
            ttn = vts(v_thn, v_th, -1.0, ALU.mult, deps=[tth], sig=True)
            return [tgm, tth, ttn]

        def phase_D(half, tth):
            joinD = s.join()
            X1T = PX.ap()[:, 0:4096]
            FT = PX.ap()[:, 4096:8192]
            U0 = PH.ap()[:, 16384:20480]
            UU = PH.ap()[:, 20480:24576]
            GG = PH.ap()[:, 24576:28672]
            BMAT = PW.ap()[:, 1024:1536]
            last_t = None
            for tb in range(4):
                tx1 = dma("sync", X1T,
                          x1_sc[half * 512 + tb * 128:
                                half * 512 + (tb + 1) * 128, :],
                          deps=(joinD if tb == 0 else [last_t]), ch="X0")
                tf, _, _ = layernorm(X1T, FT, [tx1],
                                     rstd=rstd3[half * 4 + tb],
                                     biasv=bias3[half * 4 + tb])
                tsq = None
                for i in range(4):
                    tsq = act(sq_ps, FT[:, i * 1024:(i + 1) * 1024],
                              ACTF.Square, accum=v_ssa[i], deps=[tf],
                              sig=(i == 3))
                d1 = vtt(v_ssa[0], v_ssa[0], v_ssa[1], ALU.add, deps=[tsq],
                         sig=True)
                d2 = vtt(v_ssa[2], v_ssa[2], v_ssa[3], ALU.add, deps=[tsq],
                         sig=True)
                d3 = vtt(v_ssa[0], v_ssa[0], v_ssa[2], ALU.add,
                         deps=[d1, d2], sig=True)
                tss = vts(v_ssa[0], v_ssa[0], 1.0 / 4096.0, ALU.mult,
                          deps=[d3], sig=True)
                tr3 = act(v_r3, v_ssa[0], ACTF.Sqrt, bias=v_eps, deps=[tss],
                          sig=True)
                tr3 = vrecip(v_r3, v_r3, deps=[tr3], sig=True)
                tam = vred(v_am, FT, ALU.max, deps=[tf], absv=True, sig=True)
                d4 = vtt(v_am, v_am, v_r3, ALU.mult, deps=[tam, tr3],
                         sig=True)
                tc = vts(v_am, v_am, EPS, ALU.max, deps=[d4], sig=True)
                t5 = vts(v_rcd, v_am, 1.0 / 127.0, ALU.mult, deps=[tc],
                         sig=True)
                tsvg = vtt(svg3[half * 4 + tb], v_rcd, v_gmc, ALU.mult,
                           deps=[t5] + tth, sig=True)
                trcd = vrecip(v_rcd, v_am, deps=[tc, tsvg], sig=True)
                tsv = vstt(v_rcd, v_rcd, 127.0, v_r3, ALU.mult, ALU.mult,
                           deps=[trcd, tr3], sig=True)
                tu0 = act(U0, FT, ACTF.Identity, bias=v_1285, scale=v_rcd,
                          deps=[tsv, last_t], sig=True)
                tu = vcopy(UU, U0, deps=[tu0], sig=True)
                for bit in (128.0, 64.0, 32.0, 16.0, 8.0, 4.0, 2.0, 1.0):
                    tg = vts(GG, UU, bit, ALU.is_ge, deps=[tu], sig=True)
                    tu = vstt(UU, GG, -bit, UU, ALU.mult, ALU.add, deps=[tg],
                              sig=True)
                trr = vstt(GG, U0, -128.0, UU, ALU.add, ALU.subtract,
                           deps=[tu, tu0], sig=True)
                for g4 in range(8):
                    ttp = None
                    for j in range(4):
                        kc = g4 * 4 + j
                        ttp = tp(PS2[:, 1024 + j * 128:1152 + j * 128],
                                 GG[:, kc * 128:(kc + 1) * 128],
                                 deps=([trr, last_t] if last_t else [trr]),
                                 sig=(j == 3))
                    dst = PH.ap().rearrange("p (c m) -> p c m", m=512)[
                        :, g4 * 4:(g4 + 1) * 4, tb * 128:(tb + 1) * 128]
                    last_t = vcopy(dst, PS2[:, 1024:1536].rearrange(
                        "p (a m) -> p a m", a=4), deps=[ttp], sig=True)
            txq_done = last_t

            G1 = PW.ap()[:, 2048:2560]
            wtok = [None, None]
            ttr = [None, None]
            xconv = [None, None]
            last_ep = [None, None]
            geprev = None
            for ms in range(8):
                psA = PS1 if ms % 2 == 0 else PS2
                tbm = dma("sync", BMAT, bfull[:, ms * 512:(ms + 1) * 512],
                          deps=[geprev] if geprev else [txq_done], ch="BM")
                lastmm = [None] * 4
                for ks in range(32):
                    slot = ks % 2
                    piece = wft[ks * 128:(ks + 1) * 128,
                                ms * 512:(ms + 1) * 512]
                    d = [txq_done] + tth
                    if ttr[slot] is not None:
                        d.append(ttr[slot])
                    tw = dma("sync", WBUF[slot], piece, deps=d,
                         ch=("W0", "W1")[slot])
                    dtern = [tw] + tth
                    if wtok[slot] is not None:
                        dtern.append(wtok[slot])
                    tg1 = vts(G1, WBUF[slot], v_th, ALU.is_ge, deps=dtern,
                              sig=True)
                    tg2 = vts(TSTB[slot], WBUF[slot], v_thn, ALU.is_le,
                              deps=[tw, wtok[slot]], sig=True)
                    tg3 = vtt(TSTB[slot], G1, TSTB[slot], ALU.subtract,
                              deps=[tg1, tg2], sig=True)
                    ttr[slot] = tg3
                    dcv = [txq_done]
                    if xconv[slot] is not None:
                        dcv.append(xconv[slot])
                    tcv = act(XQB[slot],
                              PH.ap()[:, ks * 512:(ks + 1) * 512],
                              ACTF.Identity, deps=dcv, sig=True)
                    lm = None
                    for tb in range(4):
                        dd = [tg3, tcv]
                        if ks == 0:
                            dd.append(last_ep[ms % 2])
                        lm = mm(psA[:, tb * 512:(tb + 1) * 512],
                                XQB[slot][:, tb * 128:(tb + 1) * 128],
                                TSTB[slot], start=(ks == 0), stop=(ks == 31),
                                deps=dd, sig=(tb == 3))
                        lastmm[tb] = lm
                    wtok[slot] = lm
                    xconv[slot] = lm
                tep = None
                for tb in range(4):
                    tyl = vstt(T1A, psA[:, tb * 512:(tb + 1) * 512],
                               svg3[half * 4 + tb], BMAT, ALU.mult, ALU.add,
                               deps=[lastmm[3], tbm, geprev], sig=True)
                    geprev = act(PH3_4096[:, 4 + tb,
                                          ms * 512:(ms + 1) * 512],
                                 T1A, ACTF.Gelu, deps=[tyl], sig=True)
                    tep = geprev
                last_ep[ms % 2] = tep
            last_tr = tep

            tlast = None
            for tb in range(4):
                gt = PH3_4096[:, 4 + tb, :]
                tx1 = dma("sync", X1T,
                          x1_sc[half * 512 + tb * 128:
                                half * 512 + (tb + 1) * 128, :],
                          deps=([last_tr] if tb == 0 else [tlast]), ch="X0")
                tfa = act(FT, X1T, ACTF.Identity, bias=bias3[half * 4 + tb],
                          scale=rstd3[half * 4 + tb], deps=[tx1, last_tr],
                          sig=True)
                tsum = vtt(FT, FT, gt, ALU.add, deps=[tfa, last_tr], sig=True)
                tap, _, _ = layernorm(FT, FT, [tsum])
                tout = vtt(FT, FT, X1T, ALU.add, deps=[tap], sig=True)
                tlast = dma("gpsimd",
                            out[half * 512 + tb * 128:
                                half * 512 + (tb + 1) * 128, :],
                            FT, deps=[tout], ch="ST")
            return tlast

        phase_A(0)
        phase_B(0)
        phase_A(1)
        phase_B(1)
        phase_C(0)
        phase_C(1)
        tth = phase_D0()
        phase_D(0, tth)
        phase_D(1, tth)

        s.op("vector", (lambda e: e.engine_nop()),
             deps=[("S0", s.cnt["S0"]), ("S1", s.cnt["S1"])])

        with nc.Block() as blk:
            def make(engname):
                def run(eng):
                    for wl, emit, tok in s.stream[engname]:
                        for (sm, v) in wl:
                            eng.wait_ge(sem[sm], v)
                        inst = emit(eng)
                        if tok is not None:
                            inst.then_inc(sem[tok[0]],
                                          16 if tok[0] in DMA_CHANNELS else 1)
                return run

            blk.sync(make("sync"))
            blk.gpsimd(make("gpsimd"))
            blk.tensor(make("tensor"))
            blk.vector(make("vector"))
            blk.scalar(make("scalar"))

    return nc


def _host_tables():
    HEAD_DIM = 128
    ROPE_THETA = 10000.0
    inv_freq = 1.0 / ROPE_THETA ** (np.arange(0, HEAD_DIM, 2,
                                              dtype=np.float32) / HEAD_DIM)
    pos = np.arange(S, dtype=np.float32)
    freqs = np.outer(pos, inv_freq)
    emb = np.concatenate([freqs, freqs], axis=-1)
    cos = np.cos(emb).astype(np.float32)
    sin = np.sin(emb).astype(np.float32)
    return cos, sin


def kernel(L0_attn_hidden_states, L0_attn_Wq, L0_attn_Wkr, L0_attn_Wdk,
           L0_attn_Wupk, L0_attn_Wupv, L0_attn_Wo, L0_ffn_W, L0_ffn_b):
    X = np.ascontiguousarray(np.asarray(L0_attn_hidden_states, np.float32))
    WqT = np.ascontiguousarray(np.asarray(L0_attn_Wq, np.float32).T)
    WkrT = np.ascontiguousarray(np.asarray(L0_attn_Wkr, np.float32).T)
    WdkT = np.ascontiguousarray(np.asarray(L0_attn_Wdk, np.float32).T)
    WupkT = np.ascontiguousarray(np.asarray(L0_attn_Wupk, np.float32).T)
    WupvT = np.ascontiguousarray(np.asarray(L0_attn_Wupv, np.float32).T)
    WoT = np.ascontiguousarray(np.asarray(L0_attn_Wo, np.float32).T)
    WfT = np.ascontiguousarray(np.asarray(L0_ffn_W, np.float32).T)
    bfl = np.ascontiguousarray(np.broadcast_to(
        np.asarray(L0_ffn_b, np.float32)[None, :], (128, D)))

    cos, sin = _host_tables()
    ident = np.eye(128, dtype=np.float32)
    ones = np.ones((128, 128), dtype=np.float32)
    diag = np.where(np.arange(128)[None, :] <= np.arange(128)[:, None],
                    np.float32(0.0), np.float32(NEG1E9)).astype(np.float32)
    pqm = np.zeros((128, 128), np.float32)
    for i in range(64):
        pqm[i, 64 + i] = 1.0
        pqm[64 + i, i] = 1.0
    pkm = np.zeros((128, 128), np.float32)
    for base in (0, 64):
        for i in range(32):
            pkm[base + i, base + 32 + i] = 1.0
            pkm[base + 32 + i, base + i] = 1.0

    cosq_rows = np.ascontiguousarray(cos.T)
    sinq_rows = sin.T.copy()
    sinq_rows[0:64] *= -1.0
    kmap = np.zeros(128, dtype=np.int64)
    for r in range(128):
        rr = r % 64
        kmap[r] = rr if rr < 32 else 64 + (rr - 32)
    cosk_rows = np.ascontiguousarray(cos.T[kmap])
    sink_rows = sin.T[kmap].copy()
    for r in range(128):
        if (r % 64) < 32:
            sink_rows[r] *= -1.0

    nc = build_program()
    in_maps = []
    own_blocks_all = []
    for c in range(8):
        b = c // 2
        odd = c % 2
        h0 = [4, 5, 6, 7, 0, 1, 2, 3] if odd else [0, 1, 2, 3, 4, 5, 6, 7]
        h1 = ([8, 9, 10, 11, 12, 13, 14, 15] if odd
              else [12, 13, 14, 15, 8, 9, 10, 11])
        own_blocks_all.append(h0[:4] + h1[:4])
        perm0 = np.concatenate([np.arange(blk * 128, (blk + 1) * 128)
                                for blk in h0])
        perm1 = np.concatenate([np.arange(blk * 128, (blk + 1) * 128)
                                for blk in h1])
        xb = X[b]
        tail = np.zeros((128, 2), np.float32)
        tail[:, 0 if not odd else 1] = np.float32(NEG1E9)
        m = {
            "x_h0": np.ascontiguousarray(xb[perm0]),
            "x_h1": np.ascontiguousarray(xb[perm1]),
            "x_own": np.ascontiguousarray(
                np.concatenate([xb[perm0[:512]], xb[perm1[:512]]])),
            "wqt": WqT, "wkrt": WkrT, "wdkt": WdkT, "wupkt": WupkT,
            "wupvt": WupvT, "wot": WoT, "wft": WfT, "bfull": bfl,
            "identc": ident, "diagm": diag, "pq": pqm, "pk": pkm,
            "onesc": ones, "tailmask": tail,
            "cosq0": np.ascontiguousarray(cosq_rows[:, perm0[:512]]),
            "sinq0": np.ascontiguousarray(sinq_rows[:, perm0[:512]]),
            "cosq1": np.ascontiguousarray(cosq_rows[:, perm1[:512]]),
            "sinq1": np.ascontiguousarray(sinq_rows[:, perm1[:512]]),
            "cosk0": np.ascontiguousarray(cosk_rows[:, perm0]),
            "sink0": np.ascontiguousarray(sink_rows[:, perm0]),
            "cosk1": np.ascontiguousarray(cosk_rows[:, perm1]),
            "sink1": np.ascontiguousarray(sink_rows[:, perm1]),
        }
        in_maps.append(m)

    res = run_bass_kernel_spmd(nc, in_maps, core_ids=list(range(8)))
    full = np.empty((B, S, D), np.float32)
    for c in range(8):
        b = c // 2
        o = np.asarray(res.results[c]["out"]).reshape(8, 128, D)
        for i, blk in enumerate(own_blocks_all[c]):
            full[b, blk * 128:(blk + 1) * 128, :] = o[i]
    return full
